# version 03
# baseline (speedup 1.0000x reference)
import contextlib
import os

import numpy as np

import concourse.bass as bass
import concourse.mybir as mybir
from concourse.alu_op_type import AluOpType
from concourse.bass_utils import run_bass_kernel_spmd

F32 = mybir.dt.float32
BF16 = mybir.dt.bfloat16
I32 = mybir.dt.int32
AF = mybir.ActivationFunctionType
AX = mybir.AxisListType
OP = AluOpType

N_CORES = 8
B_CORE = 2
S = 2048
D = 1024
DI = 2048
NST = 16
RANK = 64
PTOK = 512
NCH = B_CORE * S // PTOK
NDCH = DI // 128
NOCH = D // 128
FFI = 2816
FF_OT = FFI // 128
TWO_PI = 6.283185307179586
INV_TWO_PI = 1.0 / TWO_PI

ENGS = ("sync", "tensor", "vector", "scalar", "gpsimd")
SEM_NAMES = ("s_tensor", "s_scalar", "s_vector", "s_gpsimd",
             "s_dx", "s_dw", "s_dout", "s_dscr", "s_dg", "s_dg2")


class Tok:
    __slots__ = ("sem", "val", "eng", "dma")

    def __init__(self, sem, val, eng, dma):
        self.sem, self.val, self.eng, self.dma = sem, val, eng, dma


class Buf:
    __slots__ = ("name", "w", "rs")

    def __init__(self, name="?"):
        self.name = name
        self.w = None
        self.rs = []


class Sched:
    def __init__(self, nc):
        self.nc = nc
        self.q = {e: [] for e in ENGS}
        self.cnt = {s: 0 for s in SEM_NAMES}
        self.waited = {}
        self.fence_vals = {}
        self.handles = None

    def fence(self):
        self.fence_vals = dict(self.cnt)

    def emit(self, eng, fn, reads=(), writes=(), dma_stream=None):
        is_dma = dma_stream is not None
        sem = dma_stream if is_dma else "s_" + eng
        waits = {}

        def add_tok(t):
            if t is None:
                return
            if t.eng == eng and not t.dma and not is_dma:
                return
            if waits.get(t.sem, -1) < t.val:
                waits[t.sem] = t.val

        if is_dma and self.cnt[sem] > 0:
            if waits.get(sem, -1) < self.cnt[sem]:
                waits[sem] = self.cnt[sem]
        for b in reads:
            add_tok(b.w)
        for b in writes:
            add_tok(b.w)
            for t in b.rs:
                add_tok(t)
        for s, v in self.fence_vals.items():
            if s == "s_" + eng and not is_dma:
                continue
            if waits.get(s, -1) < v:
                waits[s] = v
        real_waits = []
        for s, v in waits.items():
            key = (eng, s)
            if self.waited.get(key, 0) >= v:
                continue
            self.waited[key] = v
            real_waits.append((s, v))
        amt = 16 if is_dma else 1
        self.cnt[sem] += amt
        tok = Tok(sem, self.cnt[sem], eng, is_dma)
        self.q[eng].append((real_waits, fn, sem, amt))
        for b in writes:
            b.w = tok
            b.rs = []
        for b in reads:
            b.rs.append(tok)
        return tok

    def wait_all(self, eng):
        snap = dict(self.cnt)

        def fn(e):
            return None
        real_waits = []
        for s, v in snap.items():
            if s == "s_" + eng:
                continue
            key = (eng, s)
            if self.waited.get(key, 0) >= v:
                continue
            self.waited[key] = v
            real_waits.append((s, v))
        self.q[eng].append((real_waits, None, None, 0))

    def flush(self, block):
        handles = self.handles
        for eng_name in ENGS:
            ops = self.q[eng_name]
            if not ops:
                continue
            self.q[eng_name] = []
            section = getattr(block, eng_name)

            def make(ops_local):
                def body(E):
                    for waits, fn, sem, amt in ops_local:
                        for s, v in waits:
                            E.wait_ge(handles[s], v)
                        if fn is not None:
                            ins = fn(E)
                            if sem is not None:
                                ins.then_inc(handles[sem], amt)
                return body
            section(make(ops))





def emit_T(sch, t, items, f32=False):
    if f32:
        banks = (("pTf", 4), ("pb0", 4))
        ident, identB = t["ident_f"], t["ident_fB"]
    else:
        banks = (("pT16a", 8), ("pT16b", 8))
        ident, identB = t["ident_b"], t["ident_bB"]
    idx = 0
    gi = 0
    while idx < len(items):
        bname, slots = banks[gi % 2]
        bank, bankB = t[bname], t[bname + "B"]
        group = items[idx:idx + slots]
        for si, (in_ap, in_bufs, eeng, mk, erd, ewr) in enumerate(group):
            sch.emit("tensor",
                     (lambda E, in_ap=in_ap, si=si, bank=bank, ident=ident:
                      E.transpose(out=bank[:, si * 128:(si + 1) * 128],
                                  in_=in_ap, identity=ident[:])),
                     reads=list(in_bufs) + [identB], writes=[bankB])
        for si, (in_ap, in_bufs, eeng, mk, erd, ewr) in enumerate(group):
            slot_ap = bank[:, si * 128:(si + 1) * 128]
            sch.emit(eeng, mk(slot_ap), reads=list(erd) + [bankB],
                     writes=list(ewr))
        idx += slots
        gi += 1

def emit_quant(sch, tiles, src_ap, out_ap, K, scr):
    nblk = K // 16
    v = "vector"
    s = "scalar"
    src3 = src_ap.rearrange("p (b k) -> p b k", k=16)
    bm, bmB = scr["bm"]
    sc, scB = scr["sc"]
    rs, rsB = scr["rs"]
    xs, xsB = scr["xs"]
    ab, abB = scr["ab"]
    i1, i1B = scr["i1"]
    i2, i2B = scr["i2"]
    m2, m2B = scr["m2"]
    m4, m4B = scr["m4"]
    sS, sB = scr["s"]
    isv, isvB = scr["isv"]
    srcB = scr["_srcB"]
    outB = scr["_outB"]
    sch.emit(v, lambda E: E.tensor_reduce(out=bm[:, 0:nblk], in_=src3, axis=AX.X,
                                          op=OP.max, apply_absolute_value=True),
             reads=[srcB], writes=[bmB])
    sch.emit(v, lambda E: E.tensor_scalar(out=sc[:, 0:nblk], in0=bm[:, 0:nblk],
                                          scalar1=1.0 / 6.0, scalar2=1e-12,
                                          op0=OP.mult, op1=OP.max),
             reads=[bmB], writes=[scB])
    sch.emit(v, lambda E: E.reciprocal(out=rs[:, 0:nblk], in_=sc[:, 0:nblk]),
             reads=[scB], writes=[rsB])
    rs3 = rs[:, 0:nblk].unsqueeze(2).broadcast_to([128, nblk, 16])
    xs3 = xs[:, 0:K].rearrange("p (b k) -> p b k", k=16)
    sch.emit(v, lambda E: E.tensor_tensor(out=xs3, in0=src3, in1=rs3, op=OP.mult),
             reads=[srcB, rsB], writes=[xsB])
    sch.emit(s, lambda E: E.activation(out=ab[:, 0:K], in_=xs[:, 0:K], func=AF.Abs),
             reads=[xsB], writes=[abB])
    sch.emit(s, lambda E: E.activation(out=i1[:, 0:K], in_=ab[:, 0:K],
                                       func=AF.Copy, scale=0.5, bias=-0.5),
             reads=[abB], writes=[i1B])
    sch.emit(s, lambda E: E.activation(out=i2[:, 0:K], in_=ab[:, 0:K],
                                       func=AF.Copy, scale=0.25, bias=-0.5),
             reads=[abB], writes=[i2B])
    sch.emit(v, lambda E: E.tensor_scalar(out=m2[:, 0:K], in0=i1[:, 0:K],
                                          scalar1=1.0, scalar2=0.5,
                                          op0=OP.min, op1=OP.mult),
             reads=[i1B], writes=[m2B])
    sch.emit(v, lambda E: E.tensor_scalar(out=m4[:, 0:K], in0=i2[:, 0:K],
                                          scalar1=1.0, scalar2=0.5,
                                          op0=OP.min, op1=OP.add),
             reads=[i2B], writes=[m4B])
    sch.emit(v, lambda E: E.tensor_tensor(out=sS[:, 0:K], in0=m2[:, 0:K],
                                          in1=m4[:, 0:K], op=OP.add),
             reads=[m2B, m4B], writes=[sB])
    sch.emit(v, lambda E: E.reciprocal(out=isv[:, 0:K], in_=sS[:, 0:K]),
             reads=[sB], writes=[isvB])
    sch.emit(v, lambda E: E.tensor_tensor(out=xs[:, 0:K], in0=xs[:, 0:K],
                                          in1=isv[:, 0:K], op=OP.mult),
             reads=[xsB, isvB], writes=[xsB])
    sch.emit(s, lambda E: E.activation(out=i1[:, 0:K], in_=xs[:, 0:K],
                                       func=AF.Copy, bias=0.0),
             reads=[xsB], writes=[i1B])
    sch.emit(v, lambda E: E.tensor_tensor(out=m2[:, 0:K], in0=i1[:, 0:K],
                                          in1=sS[:, 0:K], op=OP.mult),
             reads=[i1B, sB], writes=[m2B])
    q3 = m2[:, 0:K].rearrange("p (b k) -> p b k", k=16)
    sc3 = sc[:, 0:nblk].unsqueeze(2).broadcast_to([128, nblk, 16])
    out3 = out_ap.rearrange("p (b k) -> p b k", k=16)
    sch.emit(v, lambda E: E.tensor_tensor(out=out3, in0=q3, in1=sc3, op=OP.mult),
             reads=[m2B, scB], writes=[outB])


def emit_ln_fm(sch, t, src_ap, srcB, out_ap, outB, noch, gcol=None, gcolB=None,
               bcol=None, bcolB=None):
    for o in range(noch):
        oo = o
        sch.emit("tensor", lambda E, oo=oo: E.matmul(
            out=t["pstatA"][0:1, :], lhsT=t["ones_col_f"][:, 0:1],
            rhs=src_ap[:, oo * 512:(oo + 1) * 512],
            start=(oo == 0), stop=(oo == noch - 1)),
            reads=[srcB, t["ones_col_fB"]], writes=[t["pstatAB"]])
    nd0 = float(noch * 128)
    sch.emit("vector", lambda E: E.tensor_scalar(
        out=t["st_mu"][0:1, :], in0=t["pstatA"][0:1, :], scalar1=1.0 / nd0,
        scalar2=None, op0=OP.mult), reads=[t["pstatAB"]], writes=[t["st_muB"]])
    for o in range(noch):
        oo = o
        sch.emit("scalar", lambda E, oo=oo: E.activation(
            out=t["lnsq"][:, oo * 512:(oo + 1) * 512],
            in_=src_ap[:, oo * 512:(oo + 1) * 512], func=AF.Square),
            reads=[srcB], writes=[t["lnsqB"]])
    for o in range(noch):
        oo = o
        sch.emit("tensor", lambda E, oo=oo: E.matmul(
            out=t["pstatA"][0:1, :], lhsT=t["ones_col_b"][:, 0:1],
            rhs=t["lnsq"][:, oo * 512:(oo + 1) * 512],
            start=(oo == 0), stop=(oo == noch - 1)),
            reads=[t["lnsqB"], t["ones_col_bB"]], writes=[t["pstatAB"]])
    nd = float(noch * 128)
    sch.emit("vector", lambda E: E.tensor_scalar(
        out=t["st_m2"][0:1, :], in0=t["pstatA"][0:1, :], scalar1=1.0 / nd,
        scalar2=None, op0=OP.mult), reads=[t["pstatAB"]], writes=[t["st_m2B"]])
    sch.emit("vector", lambda E: E.tensor_tensor(
        out=t["st_v"][0:1, :], in0=t["st_mu"][0:1, :], in1=t["st_mu"][0:1, :],
        op=OP.mult), reads=[t["st_muB"]], writes=[t["st_vB"]])
    sch.emit("vector", lambda E: E.tensor_tensor(
        out=t["st_v"][0:1, :], in0=t["st_m2"][0:1, :], in1=t["st_v"][0:1, :],
        op=OP.subtract), reads=[t["st_m2B"], t["st_vB"]], writes=[t["st_vB"]])
    sch.emit("scalar", lambda E: E.activation(
        out=t["st_sd"][0:1, :], in_=t["st_v"][0:1, :], func=AF.Sqrt,
        bias=t["constv"][0:1, 2:3]),
        reads=[t["st_vB"], t["constvB"]], writes=[t["st_sdB"]])
    sch.emit("vector", lambda E: E.reciprocal(
        out=t["st_r"][0:1, :], in_=t["st_sd"][0:1, :]),
        reads=[t["st_sdB"]], writes=[t["st_rB"]])
    sch.emit("tensor", lambda E: E.matmul(
        out=t["pb_mu"][:], lhsT=t["ones_row_f"][0:1, :], rhs=t["st_mu"][0:1, :],
        start=True, stop=True),
        reads=[t["st_muB"], t["ones_row_fB"]], writes=[t["pb_muB"]])
    sch.emit("tensor", lambda E: E.matmul(
        out=t["pb_r"][:], lhsT=t["ones_row_f"][0:1, :], rhs=t["st_r"][0:1, :],
        start=True, stop=True),
        reads=[t["st_rB"], t["ones_row_fB"]], writes=[t["pb_rB"]])
    for o in range(noch):
        oo = o
        sch.emit("vector", lambda E, oo=oo: E.tensor_tensor(
            out=t["sp_a"][:], in0=src_ap[:, oo * 512:(oo + 1) * 512],
            in1=t["pb_mu"][:], op=OP.subtract),
            reads=[srcB, t["pb_muB"]], writes=[t["sp_aB"]])
        if gcol is None:
            sch.emit("vector", lambda E, oo=oo: E.tensor_tensor(
                out=out_ap[:, oo * 512:(oo + 1) * 512], in0=t["sp_a"][:],
                in1=t["pb_r"][:], op=OP.mult),
                reads=[t["sp_aB"], t["pb_rB"]], writes=[outB])
        else:
            sch.emit("vector", lambda E, oo=oo: E.tensor_tensor(
                out=t["sp_a"][:], in0=t["sp_a"][:], in1=t["pb_r"][:], op=OP.mult),
                reads=[t["sp_aB"], t["pb_rB"]], writes=[t["sp_aB"]])
            sch.emit("vector", lambda E, oo=oo: E.tensor_scalar(
                out=out_ap[:, oo * 512:(oo + 1) * 512], in0=t["sp_a"][:],
                scalar1=gcol[:, oo:oo + 1], scalar2=bcol[:, oo:oo + 1],
                op0=OP.mult, op1=OP.add),
                reads=[t["sp_aB"], gcolB, bcolB], writes=[outB])



_PHASES = set(os.environ.get("MK_PHASES", "p,a,b1,b2").split(","))


def build_program():
    nc = bass.Bass("TRN2", target_bir_lowering=False)
    sch = Sched(nc)

    dram = {}

    def din(name, shape, dtype=F32):
        dram[name] = nc.dram_tensor(name, list(shape), dtype, kind="ExternalInput").ap()
        return dram[name]

    _dbg = os.environ.get("MK_DEBUG", "") == "1"

    def dscr(name, shape, dtype):
        kind = "ExternalOutput" if _dbg else "Internal"
        dram[name] = nc.dram_tensor(name, list(shape), dtype, kind=kind).ap()
        return dram[name]

    x_ap = din("x", [B_CORE, S, D])
    w_in_ap = din("W_in", [D, 2 * DI])
    b_in_ap = din("b_in_t", [128, 32])
    conv_w_ap = din("conv_w_t", [128, 64])
    conv_b_ap = din("conv_b_t", [128, 16])
    w_xp_ap = din("W_xp", [DI, 96])
    w_dt_ap = din("W_dt", [RANK, DI])
    b_dt_ap = din("b_dt_t", [128, 16])
    a_log_ap = din("A_log_re", [128, 256])
    d_skip_ap = din("D_t", [128, 16])
    w_out_ap = din("W_out", [DI, D])
    b_out_ap = din("b_out_t", [128, 8])
    ln1_g_ap = din("ln1_g_t", [128, 8])
    ln1_b_ap = din("ln1_b_t", [128, 8])
    ln2_g_ap = din("ln2_g_t", [128, 8])
    ln2_b_ap = din("ln2_b_t", [128, 8])
    w_si_ap = din("W_si", [D, D])
    b_si_ap = din("b_si_t", [128, 8])
    gates_ap = din("gates_t", [128, 24])
    alpha_ap = din("alpha_t", [128, 8])
    w_so_ap = din("W_so", [D, D])
    b_so_ap = din("b_so_t", [128, 8])
    wg_ap = din("w_gate", [FFI, D])
    wu_ap = din("w_up", [FFI, D])
    wd_ap = din("w_down", [D, FFI])
    ident_ap = din("identity", [128, 128])
    sel_ap = din("sel", [16, 2048])
    ones_ap = din("ones", [128, 2])
    ones_row_ap = din("ones_row", [1, 128])
    constv_ap = din("constv", [128, 4])

    out_ap = nc.dram_tensor("out", [B_CORE, S, D], F32, kind="ExternalOutput").ap()
    dram["out"] = out_ap

    winbf = dscr("winbf", [128, 8, 4096], BF16)
    woutbf = dscr("woutbf", [128, 16, 1024], BF16)
    wsibf = dscr("wsibf", [128, 8, 1024], BF16)
    wsobf = dscr("wsobf", [128, 8, 1024], BF16)
    wqg = dscr("wqg", [128, 8, FFI], BF16)
    wqu = dscr("wqu", [128, 8, FFI], BF16)
    wqd = dscr("wqd", [128, FF_OT, 1024], BF16)
    g_scr = dscr("g_scr", [B_CORE, 4, 128, 8, 512], BF16)
    if _dbg:
        dbg_x = dscr("dbg_x", [128, 4096], F32)
        dbg_st = dscr("dbg_st", [128, 16], F32)
        dbg_xn = dscr("dbg_xn", [128, 4096], F32)
        dbg_xq = dscr("dbg_xq", [128, 4096], BF16)
        dbg_sg = dscr("dbg_sg", [128, FF_OT * 512], BF16)
        dbg_h = dscr("dbg_h", [128, FF_OT * 512], BF16)
        dbg_hq = dscr("dbg_hq", [128, FF_OT * 512], BF16)
    uu_scr = dscr("uu_scr", [B_CORE, 8, 128, S], BF16)
    x1_scr = dscr("x1_scr", [B_CORE, 8, 128, S], BF16)

    HB = {k: Buf("hbm_" + k) for k in
          ("winbf", "woutbf", "wsibf", "wsobf", "wqg", "wqu", "wqd",
           "g_scr", "uu_scr", "x1_scr", "out", "dbg")}

    with contextlib.ExitStack() as root:
        sems = {}
        for s in SEM_NAMES:
            sems[s] = root.enter_context(nc.semaphore(s))
        sch.handles = sems
        block = root.enter_context(nc.Block())

        def sb(name, shape, dtype=F32):
            return root.enter_context(nc.sbuf_tensor(name, shape, dtype))

        t = {}

        def add(name, tens):
            t[name] = tens
            t[name + "B"] = Buf(name)
            return tens

        def addbuf(name):
            t[name + "B"] = Buf(name)

        add("ident_f", sb("ident_f", [128, 128], F32))
        add("ident_b", sb("ident_b", [128, 128], BF16))
        add("ones_col_f", sb("ones_col_f", [128, 1], F32))
        add("ones_col_b", sb("ones_col_b", [128, 1], BF16))
        add("ones_row_f", sb("ones_row_f", [1, 128], F32))
        add("constv", sb("constv_sb", [128, 4], F32))
        add("st_mu", sb("st_mu_r", [1, 512], F32))
        add("st_m2", sb("st_m2_r", [1, 512], F32))
        add("st_v", sb("st_v_r", [1, 512], F32))
        add("st_sd", sb("st_sd_r", [1, 512], F32))
        add("st_r", sb("st_r_r", [1, 512], F32))
        add("lnsq", sb("lnsq_r", [128, 4096], BF16))
        add("sp_a", sb("sp_a_r", [128, 512], F32))

        add("pb0", root.enter_context(nc.psum_tensor("pb0", [128, 512], F32)))
        add("pb1", root.enter_context(nc.psum_tensor("pb1", [128, 512], F32)))
        add("yb0", root.enter_context(nc.psum_tensor("yb0", [128, 512], F32)))
        add("yb1", root.enter_context(nc.psum_tensor("yb1", [128, 512], F32)))
        add("pT16a", root.enter_context(nc.psum_tensor("pT16a", [128, 1024], BF16)))
        add("pT16b", root.enter_context(nc.psum_tensor("pT16b", [128, 1024], BF16)))
        add("pTf", root.enter_context(nc.psum_tensor("pTf", [128, 512], F32)))
        add("pstatA", root.enter_context(nc.psum_tensor("pstatA", [1, 512], F32)))

        sch.emit("sync", lambda E: E.dma_start(out=t["ident_f"][:], in_=ident_ap),
                 writes=[t["ident_fB"]], dma_stream="s_dx")
        sch.emit("gpsimd", lambda E: E.dma_start(out=t["ident_b"][:], in_=ident_ap),
                 writes=[t["ident_bB"]], dma_stream="s_dg")
        sch.emit("sync", lambda E: E.dma_start(
            out=t["ones_col_f"][:], in_=ones_ap[:, 0:1],
            allow_slow_non_contiguous=True),
            writes=[t["ones_col_fB"]], dma_stream="s_dx")
        sch.emit("gpsimd", lambda E: E.dma_start(
            out=t["ones_col_b"][:], in_=ones_ap[:, 0:1],
            allow_slow_non_contiguous=True),
            writes=[t["ones_col_bB"]], dma_stream="s_dg2")
        sch.emit("sync", lambda E: E.dma_start(
            out=t["ones_row_f"][:], in_=ones_row_ap),
            writes=[t["ones_row_fB"]], dma_stream="s_dx")
        sch.emit("sync", lambda E: E.dma_start(
            out=t["constv"][:], in_=constv_ap),
            writes=[t["constvB"]], dma_stream="s_dx")

        with contextlib.ExitStack() as pro:
          if _PHASES & {"p", "p1", "p2", "p2a", "p2b"}:
            def psb(name, shape, dtype=F32):
                return pro.enter_context(nc.sbuf_tensor(name, shape, dtype))

            cb0 = psb("cb0", [128, 4096], BF16)
            cb1 = psb("cb1", [128, 4096], BF16)
            cB = [Buf("cb0"), Buf("cb1")]
            jobs = []
            w_in_r = w_in_ap.rearrange("(kc p) o -> kc p o", p=128)
            for kc in range(8):
                jobs.append((w_in_r[kc], winbf[:, kc, :], 4096))
            w_out_r = w_out_ap.rearrange("(kc p) o -> kc p o", p=128)
            for kc in range(16):
                jobs.append((w_out_r[kc], woutbf[:, kc, :], 1024))
            w_si_r = w_si_ap.rearrange("(kc p) o -> kc p o", p=128)
            for kc in range(8):
                jobs.append((w_si_r[kc], wsibf[:, kc, :], 1024))
            w_so_r = w_so_ap.rearrange("(kc p) o -> kc p o", p=128)
            for kc in range(8):
                jobs.append((w_so_r[kc], wsobf[:, kc, :], 1024))
            for j, (src, dst, width) in enumerate(jobs if ("p" in _PHASES or "p1" in _PHASES) else []):
                buf = [cb0, cb1][j % 2]
                bufB = cB[j % 2]
                stream = "s_dg" if j % 2 == 0 else "s_dg2"
                sch.emit("gpsimd",
                         lambda E, src=src, buf=buf, width=width: E.dma_start(
                             out=buf[:, 0:width], in_=src),
                         writes=[bufB], dma_stream=stream)
                wb = HB["winbf"] if j < 8 else (
                    HB["woutbf"] if j < 24 else (HB["wsibf"] if j < 32 else HB["wsobf"]))
                sch.emit("sync",
                         lambda E, dst=dst, buf=buf, width=width: E.dma_start(
                             out=dst, in_=buf[:, 0:width]),
                         reads=[bufB], writes=[wb], dma_stream="s_dscr")

            do_ffnq = bool(_PHASES & {"p", "p2", "p2a", "p2b"})
            qK = 1408
            scr = {}
            for nm, dt_, wdt in (("bm", F32, 176), ("sc", F32, 176), ("rs", F32, 176),
                                 ("xs", F32, 2816), ("ab", F32, 2816),
                                 ("i1", I32, 2816), ("i2", I32, 2816),
                                 ("m2", BF16, 2816), ("m4", BF16, 2816),
                                 ("s", BF16, 2816)):
                tile = psb("q_" + nm, [128, wdt], dt_)
                scr[nm] = (tile, Buf("q_" + nm))
            scr["isv"] = scr["ab"]
            wtile = psb("wtile", [128, 2816], F32)
            wtileB = Buf("wtile")
            qout = psb("qout", [128, 2816], BF16)
            qoutB = Buf("qout")
            stg = psb("stg", [128, FF_OT * 128], BF16)
            stgB = Buf("stg")

            def quant_tile(src_ap2, out_ap2, K):
                scr2 = {k: v for k, v in scr.items()}
                scr2["_srcB"] = wtileB
                scr2["_outB"] = qoutB
                nhalf = (K + qK - 1) // qK
                for hh in range(nhalf):
                    k0 = hh * qK
                    k1 = min(K, k0 + qK)
                    emit_quant(sch, t, src_ap2[:, k0:k1], out_ap2[:, k0:k1],
                               k1 - k0, scr2)

            wg_r = wg_ap.rearrange("(ot p) k -> ot p k", p=128)
            wu_r = wu_ap.rearrange("(ot p) k -> ot p k", p=128)
            wd_r = wd_ap.rearrange("(ot p) k -> ot p k", p=128)
            for wnum, (wr, dst, hbB) in enumerate(
                    (((wg_r, wqg, HB["wqg"]), (wu_r, wqu, HB["wqu"])) if do_ffnq else ())):
                for ot in range(FF_OT):
                    sch.emit("sync", lambda E, wr=wr, ot=ot: E.dma_start(
                        out=wtile[:, 0:1024], in_=wr[ot]),
                        writes=[wtileB], dma_stream="s_dw")
                    quant_tile(wtile[:, 0:1024], qout[:, 0:1024], 1024)
                    if "p2a" in _PHASES:
                        continue
                    emit_T(sch, t, [
                        (qout[:, kb * 128:kb * 128 + 128], [qoutB], "scalar",
                         (lambda slot_ap, kb=kb: (lambda E: E.activation(
                             out=stg[:, kb * 128:kb * 128 + 128], in_=slot_ap,
                             func=AF.Copy, bias=0.0))),
                         [], [stgB]) for kb in range(8)])
                    if "p2b" in _PHASES:
                        continue
                    dst_sl = dst[:, :, ot * 128:(ot + 1) * 128]
                    stg_v = stg[:, 0:8 * 128].rearrange("p (kb o) -> p kb o", o=128)
                    sch.emit("sync", lambda E, dst_sl=dst_sl, stg_v=stg_v: E.dma_start(
                        out=dst_sl, in_=stg_v),
                        reads=[stgB], writes=[hbB], dma_stream="s_dscr")
            for ot in (range(8) if do_ffnq else ()):
                sch.emit("sync", lambda E, ot=ot: E.dma_start(
                    out=wtile[:, 0:2816], in_=wd_r[ot]),
                    writes=[wtileB], dma_stream="s_dw")
                quant_tile(wtile[:, 0:2816], qout[:, 0:2816], 2816)
                if "p2a" in _PHASES:
                    continue
                emit_T(sch, t, [
                    (qout[:, kb * 128:kb * 128 + 128], [qoutB], "scalar",
                     (lambda slot_ap, kb=kb: (lambda E: E.activation(
                         out=stg[:, kb * 128:kb * 128 + 128], in_=slot_ap,
                         func=AF.Copy, bias=0.0))),
                     [], [stgB]) for kb in range(FF_OT)])
                if "p2b" in _PHASES:
                    continue
                dst_sl = wqd[:, :, ot * 128:(ot + 1) * 128]
                stg_v = stg[:, 0:FF_OT * 128].rearrange("p (kb o) -> p kb o", o=128)
                sch.emit("sync", lambda E, dst_sl=dst_sl, stg_v=stg_v: E.dma_start(
                    out=dst_sl, in_=stg_v),
                    reads=[stgB], writes=[HB["wqd"]], dma_stream="s_dscr")
            sch.flush(block)
        sch.fence()

        with contextlib.ExitStack() as pa:
          if "a" in _PHASES:
            def asb(name, shape, dtype=F32):
                return pa.enter_context(nc.sbuf_tensor(name, shape, dtype))

            wareA = asb("wareA", [128, 8 * FFI], BF16)
            wareAB = Buf("wareA")
            x_tm = asb("a_xtm", [128, 4096], F32)
            x_tmB = Buf("a_xtm")
            x_fmA = asb("a_xfm", [128, 4096], F32)
            x_fmAB = Buf("a_xfm")
            xn_fmA = x_tm[:]
            xn_fmAB = x_tmB
            xn_tm = asb("a_xntm", [128, 4096], F32)
            xn_tmB = Buf("a_xntm")
            xq_tm = asb("a_xqtm", [128, 1024], BF16)
            xq_tmB = Buf("a_xqtm")
            xq_fm = asb("a_xqfm", [128, 8 * 512], BF16)
            xq_fmB = Buf("a_xqfm")
            aa1 = asb("a_aa1", [128, FF_OT * 512], BF16)
            aa1B = Buf("a_aa1")
            h_fm = asb("a_hfm", [128, FF_OT * 512], BF16)
            h_fmB = Buf("a_hfm")
            h_tmh = asb("a_htmh", [128, 1408], F32)
            h_tmhB = Buf("a_htmh")
            hq_tmh = asb("a_hqtmh", [128, 1408], BF16)
            hq_tmhB = Buf("a_hqtmh")
            g_tile = xq_fm
            g_tileB = xq_fmB
            scrA = {}
            for nm, dt_, wdt in (("bm", F32, 88), ("sc", F32, 88), ("rs", F32, 88),
                                 ("xs", F32, 1408), ("ab", F32, 1408),
                                 ("i1", I32, 1408), ("i2", I32, 1408),
                                 ("m2", BF16, 1408), ("m4", BF16, 1408),
                                 ("s", BF16, 1408)):
                scrA[nm] = (asb("aq_" + nm, [128, wdt], dt_), Buf("aq_" + nm))
            scrA["isv"] = scrA["ab"]

            ltA = {
                "pstatA": t["pstatA"], "pstatAB": t["pstatAB"],
                "ones_col_f": t["ones_col_f"], "ones_col_fB": t["ones_col_fB"],
                "ones_col_b": t["ones_col_b"], "ones_col_bB": t["ones_col_bB"],
                "ones_row_f": t["ones_row_f"], "ones_row_fB": t["ones_row_fB"],
                "lnsq": t["lnsq"][:], "lnsqB": t["lnsqB"],
                "st_mu": t["st_mu"], "st_muB": t["st_muB"],
                "st_m2": t["st_m2"], "st_m2B": t["st_m2B"],
                "st_v": t["st_v"], "st_vB": t["st_vB"],
                "st_sd": t["st_sd"], "st_sdB": t["st_sdB"],
                "st_r": t["st_r"], "st_rB": t["st_rB"],
                "pb_mu": t["yb0"], "pb_muB": t["yb0B"],
                "pb_r": t["yb1"], "pb_rB": t["yb1B"],
                "sp_a": t["sp_a"], "sp_aB": t["sp_aB"],
                "constv": t["constv"], "constvB": t["constvB"],
            }
            x_r = x_ap.rearrange("b (cc j p) d -> b cc p j d", j=4, p=128)
            g_w = g_scr.rearrange("b c p o t -> b c p (o t)")

            for cc in range(NCH):
                b, ci = cc // 4, cc % 4
                sch.emit("sync", lambda E, b=b, ci=ci: E.dma_start(
                    out=x_tm[:].rearrange("p (j d) -> p j d", d=1024),
                    in_=x_r[b, ci]), writes=[x_tmB], dma_stream="s_dx")
                emit_T(sch, t, [
                    (x_tm[:, tj * 1024 + o * 128:tj * 1024 + o * 128 + 128],
                     [x_tmB], "scalar",
                     (lambda slot_ap, o=o, tj=tj: (lambda E: E.activation(
                         out=x_fmA[:, o * 512 + tj * 128:o * 512 + tj * 128 + 128],
                         in_=slot_ap, func=AF.Copy, bias=0.0))),
                     [], [x_fmAB]) for o in range(8) for tj in range(4)], f32=True)
                emit_ln_fm(sch, ltA, x_fmA[:], x_fmAB, xn_fmA, xn_fmAB, 8)
                emit_T(sch, t, [
                    (xn_fmA[:, o * 512 + tj * 128:o * 512 + tj * 128 + 128],
                     [xn_fmAB], "scalar",
                     (lambda slot_ap, o=o, tj=tj: (lambda E: E.activation(
                         out=xn_tm[:, tj * 1024 + o * 128:tj * 1024 + o * 128 + 128],
                         in_=slot_ap, func=AF.Copy, bias=0.0))),
                     [], [xn_tmB]) for o in range(8) for tj in range(4)], f32=True)
                for j in range(4):
                    scr2 = {k: v for k, v in scrA.items()}
                    scr2["_srcB"] = xn_tmB
                    scr2["_outB"] = xq_tmB
                    emit_quant(sch, t, xn_tm[:, j * 1024:(j + 1) * 1024],
                               xq_tm[:, 0:1024], 1024, scr2)
                    emit_T(sch, t, [
                        (xq_tm[:, kb * 128:kb * 128 + 128], [xq_tmB], "scalar",
                         (lambda slot_ap, kb=kb, j=j: (lambda E: E.activation(
                             out=xq_fm[:, kb * 512 + j * 128:kb * 512 + j * 128 + 128],
                             in_=slot_ap, func=AF.Copy, bias=0.0))),
                         [], [xq_fmB]) for kb in range(8)])
                if _dbg and cc == 0:
                    sch.emit("sync", lambda E: E.dma_start(out=dram["dbg_xn"], in_=xn_tm[:]),
                             reads=[xn_tmB], writes=[HB["dbg"]], dma_stream="s_dout")
                    sch.emit("sync", lambda E: E.dma_start(out=dram["dbg_xq"], in_=xq_fm[:]),
                             reads=[xq_fmB], writes=[HB["dbg"]], dma_stream="s_dout")
                wareA_v = wareA[:].rearrange("p (kc o) -> p kc o", o=FFI)
                sch.emit("sync", lambda E, wareA_v=wareA_v: E.dma_start(
                    out=wareA_v, in_=wqg[:, :, :]),
                    reads=[HB["wqg"]], writes=[wareAB], dma_stream="s_dw")
                for ot in range(FF_OT):
                    pb = t["pb0"] if ot % 2 == 0 else t["pb1"]
                    pbB = t["pb0B"] if ot % 2 == 0 else t["pb1B"]
                    for kc in range(8):
                        sch.emit("tensor", lambda E, ot=ot, kc=kc, pb=pb: E.matmul(
                            out=pb[:],
                            lhsT=wareA[:, kc * FFI + ot * 128:kc * FFI + ot * 128 + 128],
                            rhs=xq_fm[:, kc * 512:(kc + 1) * 512],
                            start=(kc == 0), stop=(kc == 7)),
                            reads=[wareAB, xq_fmB], writes=[pbB])
                    sch.emit("scalar", lambda E, ot=ot, pb=pb: E.activation(
                        out=aa1[:, ot * 512:(ot + 1) * 512], in_=pb[:], func=AF.Silu),
                        reads=[pbB], writes=[aa1B])
                if _dbg and cc == 0:
                    sch.emit("sync", lambda E: E.dma_start(out=dram["dbg_sg"], in_=aa1[:]),
                             reads=[aa1B], writes=[HB["dbg"]], dma_stream="s_dout")
                sch.emit("sync", lambda E, wareA_v=wareA_v: E.dma_start(
                    out=wareA_v, in_=wqu[:, :, :]),
                    reads=[HB["wqu"]], writes=[wareAB], dma_stream="s_dw")
                for ot in range(FF_OT):
                    pb = t["pb0"] if ot % 2 == 0 else t["pb1"]
                    pbB = t["pb0B"] if ot % 2 == 0 else t["pb1B"]
                    for kc in range(8):
                        sch.emit("tensor", lambda E, ot=ot, kc=kc, pb=pb: E.matmul(
                            out=pb[:],
                            lhsT=wareA[:, kc * FFI + ot * 128:kc * FFI + ot * 128 + 128],
                            rhs=xq_fm[:, kc * 512:(kc + 1) * 512],
                            start=(kc == 0), stop=(kc == 7)),
                            reads=[wareAB, xq_fmB], writes=[pbB])
                    sch.emit("vector", lambda E, ot=ot, pb=pb, pbB=pbB: E.tensor_tensor(
                        out=h_fm[:, ot * 512:(ot + 1) * 512], in0=aa1[:, ot * 512:(ot + 1) * 512],
                        in1=pb[:], op=OP.mult),
                        reads=[pbB, aa1B], writes=[h_fmB])
                if _dbg and cc == 0:
                    sch.emit("sync", lambda E: E.dma_start(out=dram["dbg_h"], in_=h_fm[:]),
                             reads=[h_fmB], writes=[HB["dbg"]], dma_stream="s_dout")
                for j in range(4):
                    for hh in range(2):
                        ots = range(11) if hh == 0 else range(11, FF_OT)
                        emit_T(sch, t, [
                            (h_fm[:, ot * 512 + j * 128:ot * 512 + j * 128 + 128],
                             [h_fmB], "vector",
                             (lambda slot_ap, idx=idx: (lambda E: E.tensor_copy(
                                 out=h_tmh[:, idx * 128:idx * 128 + 128],
                                 in_=slot_ap))),
                             [], [h_tmhB]) for idx, ot in enumerate(ots)])
                        scr2 = {k: v for k, v in scrA.items()}
                        scr2["_srcB"] = h_tmhB
                        scr2["_outB"] = hq_tmhB
                        emit_quant(sch, t, h_tmh[:, 0:1408], hq_tmh[:, 0:1408],
                                   1408, scr2)
                        emit_T(sch, t, [
                            (hq_tmh[:, idx * 128:idx * 128 + 128], [hq_tmhB],
                             "scalar",
                             (lambda slot_ap, ot=ot, j=j: (lambda E: E.activation(
                                 out=aa1[:, ot * 512 + j * 128:ot * 512 + j * 128 + 128],
                                 in_=slot_ap, func=AF.Copy, bias=0.0))),
                             [], [aa1B]) for idx, ot in enumerate(ots)])
                if _dbg and cc == 0:
                    sch.emit("sync", lambda E: E.dma_start(out=dram["dbg_hq"], in_=aa1[:]),
                             reads=[aa1B], writes=[HB["dbg"]], dma_stream="s_dout")
                wareD_v = wareA[:, 0:FF_OT * 1024].rearrange("p (kc o) -> p kc o", o=1024)
                sch.emit("sync", lambda E, wareD_v=wareD_v: E.dma_start(
                    out=wareD_v, in_=wqd[:, :, :]),
                    reads=[HB["wqd"]], writes=[wareAB], dma_stream="s_dw")
                for o in range(8):
                    pb = t["pb0"] if o % 2 == 0 else t["pb1"]
                    pbB = t["pb0B"] if o % 2 == 0 else t["pb1B"]
                    for kc in range(FF_OT):
                        sch.emit("tensor", lambda E, o=o, kc=kc, pb=pb: E.matmul(
                            out=pb[:],
                            lhsT=wareA[:, kc * 1024 + o * 128:kc * 1024 + o * 128 + 128],
                            rhs=aa1[:, kc * 512:(kc + 1) * 512],
                            start=(kc == 0), stop=(kc == FF_OT - 1)),
                            reads=[wareAB, aa1B], writes=[pbB])
                    sch.emit("scalar", lambda E, o=o, pb=pb, pbB=pbB: E.activation(
                        out=g_tile[:, o * 512:(o + 1) * 512], in_=pb[:], func=AF.Silu),
                        reads=[pbB], writes=[g_tileB])
                sch.emit("sync", lambda E, b=b, ci=ci: E.dma_start(
                    out=g_w[b, ci], in_=g_tile[:]),
                    reads=[g_tileB], writes=[HB["g_scr"]], dma_stream="s_dout")
            sch.flush(block)
        sch.fence()

        with contextlib.ExitStack() as pb1:
          if "b1" in _PHASES:
            def bsb(name, shape, dtype=F32):
                return pb1.enter_context(nc.sbuf_tensor(name, shape, dtype))

            sel_sb = bsb("sel_sb", [16, 2048], BF16)
            sel_sbB = Buf("sel_sb")
            wdt_sb = bsb("wdt_sb", [RANK, DI], BF16)
            wdt_sbB = Buf("wdt_sb")
            wxp_sb = bsb("wxp_sb", [128, 16 * 96], BF16)
            wxp_sbB = Buf("wxp_sb")
            alog_sb = bsb("alog_sb", [128, 256], F32)
            alog_sbB = Buf("alog_sb")
            a_re = bsb("a_re", [128, 256], F32)
            a_reB = Buf("a_re")
            vec_cols = {}
            for nm, ap_, wdt in (("b_in", b_in_ap, 32), ("conv_w", conv_w_ap, 64),
                                 ("conv_b", conv_b_ap, 16), ("b_dt", b_dt_ap, 16),
                                 ("D_t", d_skip_ap, 16), ("b_out", b_out_ap, 8),
                                 ("ln1_g", ln1_g_ap, 8), ("ln1_b", ln1_b_ap, 8),
                                 ("ln2_g", ln2_g_ap, 8), ("ln2_b", ln2_b_ap, 8),
                                 ("b_si", b_si_ap, 8)):
                tile = bsb("c_" + nm, [128, wdt], F32)
                vec_cols[nm] = (tile, Buf("c_" + nm))
                sch.emit("sync", lambda E, tile=tile, ap_=ap_: E.dma_start(
                    out=tile[:], in_=ap_), writes=[vec_cols[nm][1]],
                    dma_stream="s_dx")
            sch.emit("gpsimd", lambda E: E.dma_start(out=sel_sb[:], in_=sel_ap),
                     writes=[sel_sbB], dma_stream="s_dg")
            sch.emit("gpsimd", lambda E: E.dma_start(out=wdt_sb[:], in_=w_dt_ap),
                     writes=[wdt_sbB], dma_stream="s_dg")
            wxp_v = wxp_sb[:].rearrange("p (kc o) -> p kc o", o=96)
            sch.emit("gpsimd", lambda E: E.dma_start(
                out=wxp_v, in_=w_xp_ap.rearrange("(kc p) o -> p kc o", p=128)),
                writes=[wxp_sbB], dma_stream="s_dg")
            sch.emit("sync", lambda E: E.dma_start(out=alog_sb[:], in_=a_log_ap),
                     writes=[alog_sbB], dma_stream="s_dx")
            sch.emit("scalar", lambda E: E.activation(out=a_re[:], in_=alog_sb[:],
                                                      func=AF.Exp),
                     reads=[alog_sbB], writes=[a_reB])
            sch.emit("vector", lambda E: E.tensor_scalar(
                out=a_re[:], in0=a_re[:], scalar1=-1.0, scalar2=None, op0=OP.mult),
                reads=[a_reB], writes=[a_reB])

            ar1 = bsb("ar1", [128, 8192], BF16)
            ar2 = bsb("ar2", [128, 8192], BF16)
            ar1B = Buf("ar1")
            ar2B = Buf("ar2")
            xn_fm = bsb("xn_fm", [128, 4096], F32)
            xn_fmB = Buf("xn_fm")
            ar4 = bsb("ar4", [128, 4096], BF16)
            ar4B = Buf("ar4")
            ar5 = bsb("ar5", [128, 16 * 515], BF16)
            ar5B = Buf("ar5")
            xc = bsb("xc", [128, 8192], BF16)
            xcB = Buf("xc")
            sz = bsb("sz", [128, 8192], BF16)
            szB = Buf("sz")
            ar8 = bsb("ar8", [128, 8192], BF16)
            ar8B = Buf("ar8")
            ar9 = bsb("ar9", [128, 8192], BF16)
            ar9B = Buf("ar9")
            ware = bsb("ware", [128, 8192], BF16)
            wareB = Buf("ware")
            uu_sb = ar4[:]
            uu_sbB = ar4B
            dA = [bsb("dA0", [128, 512], F32), bsb("dA1", [128, 512], F32)]
            dAB = [Buf("dA0"), Buf("dA1")]
            dxb = [bsb("dxb0", [128, 512], BF16), bsb("dxb1", [128, 512], BF16)]
            dxbB = [Buf("dxb0"), Buf("dxb1")]
            hb = [bsb("hb0", [128, 512], BF16), bsb("hb1", [128, 512], BF16)]
            hbB = [Buf("hb0"), Buf("hb1")]
            pp = [bsb("pp0", [128, 512], BF16), bsb("pp1", [128, 512], BF16)]
            ppB = [Buf("pp0"), Buf("pp1")]
            carry = bsb("carry", [128, 256], F32)
            carryB = Buf("carry")
            tails = bsb("tails", [128, 48], BF16)
            tailsB = Buf("tails")
            diag_scr = bsb("diag_scr", [128, 512], BF16)
            diag_scrB = Buf("diag_scr")
            sp_a = bsb("sp_a", [128, 512], F32)
            sp_aB = Buf("sp_a")
            sp_b = bsb("sp_b", [128, 512], F32)
            sp_bB = Buf("sp_b")
            gsc = sp_b
            gscB = sp_bB
            dtr_sb = bsb("dtr_sb", [64, 512], BF16)
            dtr_sbB = Buf("dtr_sb")
            bs_sb = bsb("bs_sb", [16, 512], BF16)
            bs_sbB = Buf("bs_sb")
            cs_sb = bsb("cs_sb", [16, 512], BF16)
            cs_sbB = Buf("cs_sb")
            lt = {
                "pstatA": t["pstatA"], "pstatAB": t["pstatAB"],
                "ones_col_f": t["ones_col_f"], "ones_col_fB": t["ones_col_fB"],
                "ones_col_b": t["ones_col_b"], "ones_col_bB": t["ones_col_bB"],
                "ones_row_f": t["ones_row_f"], "ones_row_fB": t["ones_row_fB"],
                "lnsq": t["lnsq"][:], "lnsqB": t["lnsqB"],
                "st_mu": t["st_mu"], "st_muB": t["st_muB"],
                "st_m2": t["st_m2"], "st_m2B": t["st_m2B"],
                "st_v": t["st_v"], "st_vB": t["st_vB"],
                "st_sd": t["st_sd"], "st_sdB": t["st_sdB"],
                "st_r": t["st_r"], "st_rB": t["st_rB"],
                "pb_mu": t["yb0"], "pb_muB": t["yb0B"],
                "pb_r": t["yb1"], "pb_rB": t["yb1B"],
                "sp_a": t["sp_a"], "sp_aB": t["sp_aB"],
                "constv": t["constv"], "constvB": t["constvB"],
            }
            x_r = x_ap.rearrange("b (cc j p) d -> b cc p j d", j=4, p=128)
            uu_w = uu_scr.rearrange("b i p t -> b p i t")
            x1_w = x1_scr.rearrange("b i p t -> b p i t")
            x_tm_v = ar1[:].bitcast(F32)
            x_fm_v = ar2[:].bitcast(F32)
            x1_v = ar8[:].bitcast(F32)

            for cc in range(NCH):
                b, ci = cc // 4, cc % 4
                batch_start = (ci == 0)
                sch.emit("sync", lambda E, b=b, ci=ci: E.dma_start(
                    out=x_tm_v.rearrange("p (j d) -> p j d", d=1024),
                    in_=x_r[b, ci]), writes=[ar1B], dma_stream="s_dx")
                emit_T(sch, t, [
                    (x_tm_v[:, tj * 1024 + o * 128:tj * 1024 + o * 128 + 128],
                     [ar1B], "scalar",
                     (lambda slot_ap, o=o, tj=tj: (lambda E: E.activation(
                         out=x_fm_v[:, o * 512 + tj * 128:o * 512 + tj * 128 + 128],
                         in_=slot_ap, func=AF.Copy, bias=0.0))),
                     [], [ar2B]) for o in range(8) for tj in range(4)], f32=True)
                emit_ln_fm(sch, lt, x_fm_v, ar2B, xn_fm[:], xn_fmB, 8)
                u_fm = ar4[:]
                emit_ln_fm(sch, lt, xn_fm[:], xn_fmB, u_fm, ar4B, 8,
                           gcol=vec_cols["ln1_g"][0], gcolB=vec_cols["ln1_g"][1],
                           bcol=vec_cols["ln1_b"][0], bcolB=vec_cols["ln1_b"][1])
                ware_v = ware[:].rearrange("p (kc o) -> p kc o", o=1024)
                for q in range(4):
                    sch.emit("sync", lambda E, q=q, ware_v=ware_v: E.dma_start(
                        out=ware_v, in_=winbf[:, :, q * 1024:(q + 1) * 1024]),
                        reads=[HB["winbf"]], writes=[wareB], dma_stream="s_dw")
                    for ol in range(8):
                        o = q * 8 + ol
                        pb = t["pb0"] if o % 2 == 0 else t["pb1"]
                        pbB = t["pb0B"] if o % 2 == 0 else t["pb1B"]
                        for kc in range(8):
                            sch.emit("tensor", lambda E, kc=kc, ol=ol, pb=pb: E.matmul(
                                out=pb[:],
                                lhsT=ware[:, kc * 1024 + ol * 128:kc * 1024 + ol * 128 + 128],
                                rhs=u_fm[:, kc * 512:(kc + 1) * 512],
                                start=(kc == 0), stop=(kc == 7)),
                                reads=[wareB, ar4B], writes=[pbB])
                        if o < 16:
                            sch.emit("scalar", lambda E, o=o, pb=pb, pbB=pbB: E.activation(
                                out=ar5[:, o * 515 + 3:o * 515 + 515], in_=pb[:],
                                func=AF.Identity, bias=vec_cols["b_in"][0][:, o:o + 1]),
                                reads=[pbB, vec_cols["b_in"][1]], writes=[ar5B])
                        else:
                            oz = o - 16
                            sch.emit("scalar", lambda E, o=o, oz=oz, pb=pb, pbB=pbB: E.activation(
                                out=sz[:, oz * 512:(oz + 1) * 512], in_=pb[:],
                                func=AF.Silu, bias=vec_cols["b_in"][0][:, o:o + 1]),
                                reads=[pbB, vec_cols["b_in"][1]], writes=[szB])
                if batch_start:
                    sch.emit("vector", lambda E: E.memset(tails[:], 0.0),
                             writes=[tailsB])
                for i in range(16):
                    sch.emit("vector", lambda E, i=i: E.tensor_copy(
                        out=ar5[:, i * 515:i * 515 + 3], in_=tails[:, i * 3:i * 3 + 3]),
                        reads=[tailsB], writes=[ar5B])
                for i in range(16):
                    for k in range(4):
                        sch.emit("vector", lambda E, i=i, k=k: E.tensor_scalar(
                            out=diag_scr[:, k * 128:(k + 1) * 128],
                            in0=t["ident_b"][:],
                            scalar1=vec_cols["conv_w"][0][:, i * 4 + k:i * 4 + k + 1],
                            scalar2=None, op0=OP.mult),
                            reads=[t["ident_bB"], vec_cols["conv_w"][1]],
                            writes=[diag_scrB])
                    pb = t["pb0"] if i % 2 == 0 else t["pb1"]
                    pbB = t["pb0B"] if i % 2 == 0 else t["pb1B"]
                    for k in range(4):
                        sch.emit("tensor", lambda E, i=i, k=k, pb=pb: E.matmul(
                            out=pb[:], lhsT=diag_scr[:, k * 128:(k + 1) * 128],
                            rhs=ar5[:, i * 515 + k:i * 515 + k + 512],
                            start=(k == 0), stop=(k == 3)),
                            reads=[diag_scrB, ar5B], writes=[pbB])
                    sch.emit("scalar", lambda E, i=i, pb=pb, pbB=pbB: E.activation(
                        out=xc[:, i * 512:(i + 1) * 512], in_=pb[:], func=AF.Silu,
                        bias=vec_cols["conv_b"][0][:, i:i + 1]),
                        reads=[pbB, vec_cols["conv_b"][1]], writes=[xcB])
                    sch.emit("vector", lambda E, i=i: E.tensor_copy(
                        out=tails[:, i * 3:i * 3 + 3],
                        in_=ar5[:, i * 515 + 512:i * 515 + 515]),
                        reads=[ar5B], writes=[tailsB])
                for kc in range(16):
                    sch.emit("tensor", lambda E, kc=kc: E.matmul(
                        out=t["pb0"][0:64, :],
                        lhsT=wxp_sb[:, kc * 96:kc * 96 + 64],
                        rhs=xc[:, kc * 512:(kc + 1) * 512],
                        start=(kc == 0), stop=(kc == 15)),
                        reads=[wxp_sbB, xcB], writes=[t["pb0B"]])
                for kc in range(16):
                    sch.emit("tensor", lambda E, kc=kc: E.matmul(
                        out=t["pb1"][0:16, :],
                        lhsT=wxp_sb[:, kc * 96 + 64:kc * 96 + 80],
                        rhs=xc[:, kc * 512:(kc + 1) * 512],
                        start=(kc == 0), stop=(kc == 15)),
                        reads=[wxp_sbB, xcB], writes=[t["pb1B"]])
                for kc in range(16):
                    sch.emit("tensor", lambda E, kc=kc: E.matmul(
                        out=t["pb1"][32:48, :],
                        lhsT=wxp_sb[:, kc * 96 + 80:kc * 96 + 96],
                        rhs=xc[:, kc * 512:(kc + 1) * 512],
                        start=(kc == 0), stop=(kc == 15)),
                        reads=[wxp_sbB, xcB], writes=[t["pb1B"]])
                sch.emit("scalar", lambda E: E.activation(
                    out=dtr_sb[:], in_=t["pb0"][0:64, :], func=AF.Copy, bias=0.0),
                    reads=[t["pb0B"]], writes=[dtr_sbB])
                sch.emit("scalar", lambda E: E.activation(
                    out=bs_sb[:], in_=t["pb1"][0:16, :], func=AF.Copy, bias=0.0),
                    reads=[t["pb1B"]], writes=[bs_sbB])
                sch.emit("scalar", lambda E: E.activation(
                    out=cs_sb[:], in_=t["pb1"][32:48, :], func=AF.Copy, bias=0.0),
                    reads=[t["pb1B"]], writes=[cs_sbB])
                btil = ar2[:]
                ctil = ar1[:]
                for n in range(16):
                    pb = t["pb0"] if n % 2 == 0 else t["pb1"]
                    pbB = t["pb0B"] if n % 2 == 0 else t["pb1B"]
                    sch.emit("tensor", lambda E, n=n, pb=pb: E.matmul(
                        out=pb[:], lhsT=sel_sb[:, n * 128:(n + 1) * 128],
                        rhs=bs_sb[:], start=True, stop=True),
                        reads=[sel_sbB, bs_sbB], writes=[pbB])
                    sch.emit("vector", lambda E, n=n, pb=pb, pbB=pbB: E.tensor_copy(
                        out=btil[:, n * 512:(n + 1) * 512], in_=pb[:]),
                        reads=[pbB], writes=[ar2B])
                for n in range(16):
                    pb = t["pb0"] if n % 2 == 0 else t["pb1"]
                    pbB = t["pb0B"] if n % 2 == 0 else t["pb1B"]
                    sch.emit("tensor", lambda E, n=n, pb=pb: E.matmul(
                        out=pb[:], lhsT=sel_sb[:, n * 128:(n + 1) * 128],
                        rhs=cs_sb[:], start=True, stop=True),
                        reads=[sel_sbB, cs_sbB], writes=[pbB])
                    sch.emit("vector", lambda E, n=n, pb=pb, pbB=pbB: E.tensor_copy(
                        out=ctil[:, n * 512:(n + 1) * 512], in_=pb[:]),
                        reads=[pbB], writes=[ar1B])
                dt_sb = ar8[:]
                for i in range(16):
                    pb = t["pb0"] if i % 2 == 0 else t["pb1"]
                    pbB = t["pb0B"] if i % 2 == 0 else t["pb1B"]
                    sch.emit("tensor", lambda E, i=i, pb=pb: E.matmul(
                        out=pb[:], lhsT=wdt_sb[:, i * 128:(i + 1) * 128],
                        rhs=dtr_sb[:], start=True, stop=True),
                        reads=[wdt_sbB, dtr_sbB], writes=[pbB])
                    sch.emit("scalar", lambda E, i=i, pb=pb, pbB=pbB: E.activation(
                        out=sp_a[:], in_=pb[:], func=AF.Abs,
                        bias=vec_cols["b_dt"][0][:, i:i + 1]),
                        reads=[pbB, vec_cols["b_dt"][1]], writes=[sp_aB])
                    sch.emit("scalar", lambda E: E.activation(
                        out=sp_a[:], in_=sp_a[:], func=AF.Exp, scale=-1.0),
                        reads=[sp_aB], writes=[sp_aB])
                    sch.emit("scalar", lambda E: E.activation(
                        out=sp_a[:], in_=sp_a[:], func=AF.Ln,
                        bias=t["constv"][:, 1:2]),
                        reads=[sp_aB, t["constvB"]], writes=[sp_aB])
                    sch.emit("scalar", lambda E, i=i, pb=pb, pbB=pbB: E.activation(
                        out=sp_b[:], in_=pb[:], func=AF.Relu,
                        bias=vec_cols["b_dt"][0][:, i:i + 1]),
                        reads=[pbB, vec_cols["b_dt"][1]], writes=[sp_bB])
                    sch.emit("vector", lambda E, i=i: E.tensor_tensor(
                        out=dt_sb[:, i * 512:(i + 1) * 512], in0=sp_a[:], in1=sp_b[:],
                        op=OP.add), reads=[sp_aB, sp_bB], writes=[ar8B])
                up_sb = ar9[:]
                sch.emit("vector", lambda E: E.tensor_tensor(
                    out=up_sb[:], in0=dt_sb[:], in1=xc[:], op=OP.mult),
                    reads=[ar8B, xcB], writes=[ar9B])
                yg = ar5[:]
                for i in range(16):
                    ybank = t["yb0"] if i % 2 == 0 else t["yb1"]
                    ybankB = t["yb0B"] if i % 2 == 0 else t["yb1B"]
                    for n in range(16):
                        k = i * 16 + n
                        sl = k % 2
                        sch.emit("scalar", lambda E, i=i, k=k, sl=sl: E.activation(
                            out=dA[sl][:], in_=dt_sb[:, i * 512:(i + 1) * 512],
                            func=AF.Exp, scale=a_re[:, k:k + 1]),
                            reads=[ar8B, a_reB], writes=[dAB[sl]])
                        sch.emit("vector", lambda E, i=i, n=n, sl=sl: E.tensor_tensor(
                            out=dxb[sl][:], in0=up_sb[:, i * 512:(i + 1) * 512],
                            in1=btil[:, n * 512:(n + 1) * 512], op=OP.mult),
                            reads=[ar9B, ar2B], writes=[dxbB[sl]])
                        if batch_start:
                            init = 0.0
                            rd = [dAB[sl], dxbB[sl]]
                        else:
                            init = carry[:, k:k + 1]
                            rd = [dAB[sl], dxbB[sl], carryB]
                        sch.emit("vector", lambda E, sl=sl, init=init: E.tensor_tensor_scan(
                            out=hb[sl][:], data0=dA[sl][:], data1=dxb[sl][:],
                            initial=init, op0=OP.mult, op1=OP.add),
                            reads=rd, writes=[hbB[sl]])
                        sch.emit("vector", lambda E, sl=sl, k=k: E.tensor_copy(
                            out=carry[:, k:k + 1], in_=hb[sl][:, 511:512]),
                            reads=[hbB[sl]], writes=[carryB])
                        sch.emit("gpsimd", lambda E, sl=sl, n=n: E.tensor_tensor(
                            out=pp[sl][:], in0=hb[sl][:],
                            in1=ctil[:, n * 512:(n + 1) * 512], op=OP.mult),
                            reads=[hbB[sl], ar1B], writes=[ppB[sl]])
                        sch.emit("tensor", lambda E, sl=sl, n=n, ybank=ybank: E.matmul(
                            out=ybank[:], lhsT=t["ident_b"][:], rhs=pp[sl][:],
                            start=(n == 0), stop=(n == 15)),
                            reads=[ppB[sl], t["ident_bB"]], writes=[ybankB])
                    sch.emit("gpsimd", lambda E, i=i: E.tensor_scalar(
                        out=gsc[:], in0=xc[:, i * 512:(i + 1) * 512],
                        scalar1=vec_cols["D_t"][0][:, i:i + 1], scalar2=0.0,
                        op0=OP.mult, op1=OP.add),
                        reads=[xcB, vec_cols["D_t"][1]], writes=[gscB])
                    sch.emit("vector", lambda E, ybank=ybank, ybankB=ybankB: E.tensor_tensor(
                        out=gsc[:], in0=gsc[:], in1=ybank[:], op=OP.add),
                        reads=[gscB, ybankB], writes=[gscB])
                    sch.emit("vector", lambda E, i=i: E.tensor_tensor(
                        out=yg[:, i * 515 + 3:i * 515 + 515], in0=gsc[:],
                        in1=sz[:, i * 512:(i + 1) * 512], op=OP.mult),
                        reads=[gscB, szB], writes=[ar5B])
                wo_v = ware[:].rearrange("p (kc o) -> p kc o", o=512)
                for hh in range(2):
                    sch.emit("sync", lambda E, hh=hh, wo_v=wo_v: E.dma_start(
                        out=wo_v, in_=woutbf[:, :, hh * 512:(hh + 1) * 512]),
                        reads=[HB["woutbf"]], writes=[wareB], dma_stream="s_dw")
                    for ol in range(4):
                        o = hh * 4 + ol
                        pb = t["pb0"] if o % 2 == 0 else t["pb1"]
                        pbB = t["pb0B"] if o % 2 == 0 else t["pb1B"]
                        for kc in range(16):
                            sch.emit("tensor", lambda E, kc=kc, ol=ol, pb=pb: E.matmul(
                                out=pb[:],
                                lhsT=ware[:, kc * 512 + ol * 128:kc * 512 + ol * 128 + 128],
                                rhs=yg[:, kc * 515 + 3:kc * 515 + 515],
                                start=(kc == 0), stop=(kc == 15)),
                                reads=[wareB, ar5B], writes=[pbB])
                        sch.emit("scalar", lambda E, o=o, pb=pb, pbB=pbB: E.activation(
                            out=sp_a[:], in_=pb[:], func=AF.Identity,
                            bias=vec_cols["b_out"][0][:, o:o + 1]),
                            reads=[pbB, vec_cols["b_out"][1]], writes=[sp_aB])
                        sch.emit("vector", lambda E, o=o: E.tensor_tensor(
                            out=x1_v[:, o * 512:(o + 1) * 512], in0=sp_a[:],
                            in1=xn_fm[:, o * 512:(o + 1) * 512], op=OP.add),
                            reads=[sp_aB, xn_fmB], writes=[ar8B])
                v_sb = ar9[:, 0:4096]
                emit_ln_fm(sch, lt, x1_v, ar8B, v_sb, ar9B, 8,
                           gcol=vec_cols["ln2_g"][0], gcolB=vec_cols["ln2_g"][1],
                           bcol=vec_cols["ln2_b"][0], bcolB=vec_cols["ln2_b"][1])
                wsi_v = ware[:, 0:8 * 1024].rearrange("p (kc o) -> p kc o", o=1024)
                sch.emit("sync", lambda E, wsi_v=wsi_v: E.dma_start(
                    out=wsi_v, in_=wsibf[:, :, :]),
                    reads=[HB["wsibf"]], writes=[wareB], dma_stream="s_dw")
                for o in range(8):
                    pb = t["pb0"] if o % 2 == 0 else t["pb1"]
                    pbB = t["pb0B"] if o % 2 == 0 else t["pb1B"]
                    for kc in range(8):
                        sch.emit("tensor", lambda E, kc=kc, o=o, pb=pb: E.matmul(
                            out=pb[:],
                            lhsT=ware[:, kc * 1024 + o * 128:kc * 1024 + o * 128 + 128],
                            rhs=v_sb[:, kc * 512:(kc + 1) * 512],
                            start=(kc == 0), stop=(kc == 7)),
                            reads=[wareB, ar9B], writes=[pbB])
                    sch.emit("scalar", lambda E, o=o, pb=pb, pbB=pbB: E.activation(
                        out=uu_sb[:, o * 512:(o + 1) * 512], in_=pb[:],
                        func=AF.Identity, bias=vec_cols["b_si"][0][:, o:o + 1]),
                        reads=[pbB, vec_cols["b_si"][1]], writes=[uu_sbB])
                t0 = ci * PTOK
                sch.emit("sync", lambda E, b=b, t0=t0: E.dma_start(
                    out=uu_w[b, :, :, t0:t0 + PTOK],
                    in_=uu_sb[:].rearrange("p (i tt) -> p i tt", tt=512)),
                    reads=[uu_sbB], writes=[HB["uu_scr"]], dma_stream="s_dout")
                x1b = ar9[:, 4096:8192]
                for o in range(8):
                    sch.emit("scalar", lambda E, o=o: E.activation(
                        out=x1b[:, o * 512:(o + 1) * 512],
                        in_=x1_v[:, o * 512:(o + 1) * 512], func=AF.Copy, bias=0.0),
                        reads=[ar8B], writes=[ar9B])
                sch.emit("sync", lambda E, b=b, t0=t0: E.dma_start(
                    out=x1_w[b, :, :, t0:t0 + PTOK],
                    in_=x1b.rearrange("p (i tt) -> p i tt", tt=512)),
                    reads=[ar9B], writes=[HB["x1_scr"]], dma_stream="s_dout")
            sch.flush(block)
        sch.fence()

        with contextlib.ExitStack() as pb2:
          if "b2" in _PHASES:
            def csb(name, shape, dtype=F32):
                return pb2.enter_context(nc.sbuf_tensor(name, shape, dtype))

            ware2 = csb("ware2", [128, 8 * 1024], BF16)
            ware2B = Buf("ware2")
            uuW = csb("uuW", [128, 8 * 516], BF16)
            uuWB = Buf("uuW")
            x1ld = csb("x1ld", [128, 8 * 512], BF16)
            x1ldB = Buf("x1ld")
            gld = csb("gld", [128, 8 * 512], BF16)
            gldB = Buf("gld")
            x_tm2 = csb("x_tm2", [128, 4096], F32)
            x_tm2B = Buf("x_tm2")
            duv = csb("duv", [128, 8 * 512], BF16)
            duvB = Buf("duv")
            sgp = csb("sgp", [128, 8 * 512], BF16)
            sgpB = Buf("sgp")
            out_tm = csb("out_tm", [128, 4096], F32)
            out_tmB = Buf("out_tm")
            f1 = csb("f1", [128, 512], F32)
            f2 = csb("f2", [128, 512], F32)
            f3 = csb("f3", [128, 512], F32)
            f4 = csb("f4", [128, 512], F32)
            fk = csb("fk", [128, 512], I32)
            f1B, f2B, f3B, f4B, fkB = (Buf("f1"), Buf("f2"), Buf("f3"), Buf("f4"),
                                       Buf("fk"))
            vc2 = {}
            for nm, ap_, wdt in (("alpha", alpha_ap, 8), ("gates", gates_ap, 24),
                                 ("b_so", b_so_ap, 8)):
                tile = csb("c2_" + nm, [128, wdt], F32)
                vc2[nm] = (tile, Buf("c2_" + nm))
                sch.emit("sync", lambda E, tile=tile, ap_=ap_: E.dma_start(
                    out=tile[:], in_=ap_), writes=[vc2[nm][1]], dma_stream="s_dx")
            gp = csb("c2_gp", [128, 24], F32)
            gpB = Buf("c2_gp")
            sch.emit("vector", lambda E: E.tensor_scalar(
                out=gp[:], in0=vc2["gates"][0][:], scalar1=0.1, scalar2=None,
                op0=OP.mult), reads=[vc2["gates"][1]], writes=[gpB])
            wso_v = ware2[:].rearrange("p (kc o) -> p kc o", o=1024)
            sch.emit("sync", lambda E: E.dma_start(out=wso_v, in_=wsobf[:, :, :]),
                     reads=[HB["wsobf"]], writes=[ware2B], dma_stream="s_dw")

            x_r2 = x_ap.rearrange("b (cc j p) d -> b cc p j d", j=4, p=128)
            out_r = out_ap.rearrange("b (cc j p) d -> b cc p j d", j=4, p=128)
            uu_rd = uu_scr.rearrange("b i p t -> b p i t")
            x1_rd = x1_scr.rearrange("b i p t -> b p i t")
            g_rd = g_scr.rearrange("b c p o t -> b c p (o t)")
            uuW_v = uuW[:].rearrange("p (i w) -> p i w", w=516)

            for cc in range(NCH):
                b, ci = cc // 4, cc % 4
                t0 = ci * PTOK
                lo = max(0, t0 - 2)
                hi = min(S, t0 + PTOK + 2)
                c_off = lo - (t0 - 2)
                width = hi - lo
                sch.emit("sync", lambda E, b=b, lo=lo, width=width, c_off=c_off:
                         E.dma_start(out=uuW_v[:, :, c_off:c_off + width],
                                     in_=uu_rd[b, :, :, lo:lo + width]),
                         reads=[HB["uu_scr"]], writes=[uuWB], dma_stream="s_dscr")
                if ci == 0:
                    sch.emit("vector", lambda E: E.tensor_copy(
                        out=uuW_v[:, :, 0:2],
                        in_=uuW_v[:, :, 2:3].broadcast_to([128, 8, 2])),
                        reads=[uuWB], writes=[uuWB])
                if ci == 3:
                    sch.emit("vector", lambda E: E.tensor_copy(
                        out=uuW_v[:, :, 514:516],
                        in_=uuW_v[:, :, 513:514].broadcast_to([128, 8, 2])),
                        reads=[uuWB], writes=[uuWB])
                sch.emit("sync", lambda E, b=b, t0=t0: E.dma_start(
                    out=x1ld[:].rearrange("p (i tt) -> p i tt", tt=512),
                    in_=x1_rd[b, :, :, t0:t0 + PTOK]),
                    reads=[HB["x1_scr"]], writes=[x1ldB], dma_stream="s_dscr")
                sch.emit("sync", lambda E, b=b, ci=ci: E.dma_start(
                    out=gld[:], in_=g_rd[b, ci]),
                    reads=[HB["g_scr"]], writes=[gldB], dma_stream="s_dscr")
                sch.emit("sync", lambda E, b=b, ci=ci: E.dma_start(
                    out=x_tm2[:].rearrange("p (j d) -> p j d", d=1024),
                    in_=x_r2[b, ci]), writes=[x_tm2B], dma_stream="s_dx")
                for i in range(8):
                    base = i * 516
                    C = uuW[:, base + 2:base + 514]
                    L = uuW[:, base + 1:base + 513]
                    R = uuW[:, base + 3:base + 515]
                    LL = uuW[:, base + 0:base + 512]
                    RR = uuW[:, base + 4:base + 516]
                    sch.emit("vector", lambda E, R=R, L=L: E.tensor_tensor(
                        out=f1[:], in0=R, in1=L, op=OP.subtract),
                        reads=[uuWB], writes=[f1B])
                    sch.emit("vector", lambda E, R=R, L=L: E.tensor_tensor(
                        out=f2[:], in0=R, in1=L, op=OP.add),
                        reads=[uuWB], writes=[f2B])
                    sch.emit("vector", lambda E, C=C: E.scalar_tensor_tensor(
                        out=f3[:], in0=C, scalar=-2.0, in1=f2[:],
                        op0=OP.mult, op1=OP.add),
                        reads=[uuWB, f2B], writes=[f3B])
                    sch.emit("vector", lambda E, RR=RR, LL=LL: E.tensor_tensor(
                        out=f2[:], in0=RR, in1=LL, op=OP.subtract),
                        reads=[uuWB], writes=[f2B])
                    sch.emit("vector", lambda E: E.scalar_tensor_tensor(
                        out=f2[:], in0=f2[:], scalar=0.5, in1=f1[:],
                        op0=OP.mult, op1=OP.subtract),
                        reads=[f2B, f1B], writes=[f2B])
                    sch.emit("vector", lambda E, C=C: E.tensor_tensor(
                        out=f1[:], in0=C, in1=f1[:], op=OP.mult),
                        reads=[uuWB, f1B], writes=[f1B])
                    sch.emit("vector", lambda E: E.scalar_tensor_tensor(
                        out=f1[:], in0=f1[:], scalar=-3.0, in1=f2[:],
                        op0=OP.mult, op1=OP.subtract),
                        reads=[f1B, f2B], writes=[f1B])
                    sch.emit("vector", lambda E, C=C, i=i: E.tensor_scalar(
                        out=f4[:], in0=C, scalar1=vc2["alpha"][0][:, i:i + 1],
                        scalar2=1.0, op0=OP.mult, op1=OP.add),
                        reads=[uuWB, vc2["alpha"][1]], writes=[f4B])
                    sch.emit("vector", lambda E: E.tensor_tensor(
                        out=f3[:], in0=f4[:], in1=f3[:], op=OP.mult),
                        reads=[f4B, f3B], writes=[f3B])
                    sch.emit("vector", lambda E, C=C: E.tensor_scalar(
                        out=fk[:], in0=C, scalar1=INV_TWO_PI, scalar2=None,
                        op0=OP.mult), reads=[uuWB], writes=[fkB])
                    sch.emit("vector", lambda E, C=C: E.scalar_tensor_tensor(
                        out=f4[:], in0=fk[:], scalar=-TWO_PI, in1=C,
                        op0=OP.mult, op1=OP.add),
                        reads=[fkB, uuWB], writes=[f4B])
                    sch.emit("scalar", lambda E: E.activation(
                        out=f4[:], in_=f4[:], func=AF.Sin),
                        reads=[f4B], writes=[f4B])
                    sch.emit("vector", lambda E, i=i: E.tensor_scalar(
                        out=f2[:], in0=f4[:], scalar1=gp[:, i:i + 1], scalar2=None,
                        op0=OP.mult), reads=[f4B, gpB], writes=[f2B])
                    sch.emit("vector", lambda E, i=i: E.scalar_tensor_tensor(
                        out=f2[:], in0=f1[:], scalar=gp[:, 8 + i:8 + i + 1], in1=f2[:],
                        op0=OP.mult, op1=OP.add),
                        reads=[f1B, f2B, gpB], writes=[f2B])
                    sch.emit("vector", lambda E, i=i: E.scalar_tensor_tensor(
                        out=duv[:, i * 512:(i + 1) * 512], in0=f3[:],
                        scalar=gp[:, 16 + i:16 + i + 1], in1=f2[:],
                        op0=OP.mult, op1=OP.add),
                        reads=[f3B, f2B, gpB], writes=[duvB])
                for o in range(8):
                    pb = t["pb0"] if o % 2 == 0 else t["pb1"]
                    pbB = t["pb0B"] if o % 2 == 0 else t["pb1B"]
                    for kc in range(8):
                        sch.emit("tensor", lambda E, kc=kc, o=o, pb=pb: E.matmul(
                            out=pb[:],
                            lhsT=ware2[:, kc * 1024 + o * 128:kc * 1024 + o * 128 + 128],
                            rhs=duv[:, kc * 512:(kc + 1) * 512],
                            start=(kc == 0), stop=(kc == 7)),
                            reads=[ware2B, duvB], writes=[pbB])
                    sch.emit("scalar", lambda E, o=o, pb=pb, pbB=pbB: E.activation(
                        out=f1[:], in_=pb[:], func=AF.Identity,
                        bias=vc2["b_so"][0][:, o:o + 1]),
                        reads=[pbB, vc2["b_so"][1]], writes=[f1B])
                    sch.emit("vector", lambda E, o=o: E.tensor_tensor(
                        out=f1[:], in0=f1[:], in1=x1ld[:, o * 512:(o + 1) * 512],
                        op=OP.add), reads=[f1B, x1ldB], writes=[f1B])
                    sch.emit("vector", lambda E, o=o: E.tensor_tensor(
                        out=sgp[:, o * 512:(o + 1) * 512], in0=f1[:],
                        in1=gld[:, o * 512:(o + 1) * 512], op=OP.mult),
                        reads=[f1B, gldB], writes=[sgpB])
                emit_T(sch, t, [
                    (sgp[:, o * 512 + tj * 128:o * 512 + tj * 128 + 128],
                     [sgpB], "vector",
                     (lambda slot_ap, o=o, tj=tj: (lambda E: E.tensor_tensor(
                         out=out_tm[:, tj * 1024 + o * 128:tj * 1024 + o * 128 + 128],
                         in0=slot_ap,
                         in1=x_tm2[:, tj * 1024 + o * 128:tj * 1024 + o * 128 + 128],
                         op=OP.add))),
                     [x_tm2B], [out_tmB]) for o in range(8) for tj in range(4)])
                sch.emit("sync", lambda E, b=b, ci=ci: E.dma_start(
                    out=out_r[b, ci],
                    in_=out_tm[:].rearrange("p (j d) -> p j d", d=1024)),
                    reads=[out_tmB], writes=[HB["out"]], dma_stream="s_dout")
            sch.wait_all("sync")
            sch.flush(block)

    return nc



def _col(v, k):
    return np.ascontiguousarray(v.reshape(k, 128).T.astype(np.float32))


def kernel(**inp):
    x = np.asarray(inp["L0_attn_x"], dtype=np.float32)

    sel = np.zeros((16, 16, 128), dtype=np.float32)
    for n in range(16):
        sel[n, n, :] = 1.0
    sel = sel.reshape(16, 2048)
    ones = np.ones((128, 2), dtype=np.float32)
    ident = np.eye(128, dtype=np.float32)

    gates = np.asarray(inp["L0_attn_gates"], dtype=np.float32)
    alpha = np.asarray(inp["L0_attn_alpha"], dtype=np.float32)
    gates_rep = np.concatenate([
        _col(np.repeat(gates[:, j], 64), 8) for j in range(3)], axis=1)
    alpha_rep = _col(np.repeat(alpha, 64), 8)

    a_log = np.asarray(inp["L0_attn_A_log"], dtype=np.float32)
    a_log_re = np.ascontiguousarray(
        a_log.reshape(16, 128, 16).transpose(1, 0, 2).reshape(128, 256))
    conv_w = np.asarray(inp["L0_attn_conv_w"], dtype=np.float32)
    conv_w_t = np.ascontiguousarray(
        conv_w.reshape(16, 128, 4).transpose(1, 0, 2).reshape(128, 64))

    common = {
        "W_in": np.ascontiguousarray(inp["L0_attn_W_in"], dtype=np.float32),
        "b_in_t": _col(np.asarray(inp["L0_attn_b_in"], np.float32), 32),
        "conv_w_t": conv_w_t,
        "conv_b_t": _col(np.asarray(inp["L0_attn_conv_b"], np.float32), 16),
        "W_xp": np.ascontiguousarray(inp["L0_attn_W_xp"], dtype=np.float32),
        "W_dt": np.ascontiguousarray(inp["L0_attn_W_dt"], dtype=np.float32),
        "b_dt_t": _col(np.asarray(inp["L0_attn_b_dt"], np.float32), 16),
        "A_log_re": a_log_re,
        "D_t": _col(np.asarray(inp["L0_attn_D_skip"], np.float32), 16),
        "W_out": np.ascontiguousarray(inp["L0_attn_W_out"], dtype=np.float32),
        "b_out_t": _col(np.asarray(inp["L0_attn_b_out"], np.float32), 8),
        "ln1_g_t": _col(np.asarray(inp["L0_attn_ln1_g"], np.float32), 8),
        "ln1_b_t": _col(np.asarray(inp["L0_attn_ln1_b"], np.float32), 8),
        "ln2_g_t": _col(np.asarray(inp["L0_attn_ln2_g"], np.float32), 8),
        "ln2_b_t": _col(np.asarray(inp["L0_attn_ln2_b"], np.float32), 8),
        "W_si": np.ascontiguousarray(inp["L0_attn_W_si"], dtype=np.float32),
        "b_si_t": _col(np.asarray(inp["L0_attn_b_si"], np.float32), 8),
        "gates_t": gates_rep,
        "alpha_t": alpha_rep,
        "W_so": np.ascontiguousarray(inp["L0_attn_W_so"], dtype=np.float32),
        "b_so_t": _col(np.asarray(inp["L0_attn_b_so"], np.float32), 8),
        "w_gate": np.ascontiguousarray(inp["L0_ffn_w_gate"], dtype=np.float32),
        "w_up": np.ascontiguousarray(inp["L0_ffn_w_up"], dtype=np.float32),
        "w_down": np.ascontiguousarray(inp["L0_ffn_w_down"], dtype=np.float32),
        "identity": ident,
        "sel": sel,
        "ones": ones,
        "ones_row": np.ones((1, 128), dtype=np.float32),
        "constv": np.tile(np.array([[-0.5, 1.0, 1e-5, 0.0]], dtype=np.float32),
                          (128, 1)),
    }

    nc = build_program()
    in_maps = []
    for c in range(N_CORES):
        m = dict(common)
        m["x"] = np.ascontiguousarray(x[2 * c:2 * c + 2])
        in_maps.append(m)
    res = run_bass_kernel_spmd(nc, in_maps, core_ids=list(range(N_CORES)))
    if os.environ.get("MK_DEBUG", "") == "1":
        global _last_results
        _last_results = res.results
    out = np.concatenate([np.asarray(r["out"], dtype=np.float32)
                          for r in res.results], axis=0)
    return out


# version 05
# speedup vs baseline: 1.0296x; 1.0296x over previous
import contextlib
import os

import numpy as np

import concourse.bass as bass
import concourse.mybir as mybir
from concourse.alu_op_type import AluOpType
from concourse.bass_utils import run_bass_kernel_spmd

F32 = mybir.dt.float32
BF16 = mybir.dt.bfloat16
I32 = mybir.dt.int32
AF = mybir.ActivationFunctionType
AX = mybir.AxisListType
OP = AluOpType

N_CORES = 8
B_CORE = 2
S = 2048
D = 1024
DI = 2048
NST = 16
RANK = 64
PTOK = 512
NCH = B_CORE * S // PTOK
NDCH = DI // 128
NOCH = D // 128
FFI = 2816
FF_OT = FFI // 128
TWO_PI = 6.283185307179586
INV_TWO_PI = 1.0 / TWO_PI

ENGS = ("sync", "tensor", "vector", "scalar", "gpsimd")
SEM_NAMES = ("s_tensor", "s_scalar", "s_vector", "s_gpsimd",
             "s_dx", "s_dw", "s_dout", "s_dscr", "s_dg", "s_dg2")


class Tok:
    __slots__ = ("sem", "val", "eng", "dma")

    def __init__(self, sem, val, eng, dma):
        self.sem, self.val, self.eng, self.dma = sem, val, eng, dma


class Buf:
    __slots__ = ("name", "w", "rs")

    def __init__(self, name="?"):
        self.name = name
        self.w = None
        self.rs = []


class Sched:
    def __init__(self, nc):
        self.nc = nc
        self.q = {e: [] for e in ENGS}
        self.cnt = {s: 0 for s in SEM_NAMES}
        self.waited = {}
        self.fence_vals = {}
        self.handles = None

    def fence(self):
        self.fence_vals = dict(self.cnt)

    def emit(self, eng, fn, reads=(), writes=(), dma_stream=None):
        is_dma = dma_stream is not None
        sem = dma_stream if is_dma else "s_" + eng
        waits = {}

        def add_tok(t):
            if t is None:
                return
            if t.eng == eng and not t.dma and not is_dma:
                return
            if waits.get(t.sem, -1) < t.val:
                waits[t.sem] = t.val

        if is_dma and self.cnt[sem] > 0:
            if waits.get(sem, -1) < self.cnt[sem]:
                waits[sem] = self.cnt[sem]
        for b in reads:
            add_tok(b.w)
        for b in writes:
            add_tok(b.w)
            for t in b.rs:
                add_tok(t)
        for s, v in self.fence_vals.items():
            if s == "s_" + eng and not is_dma:
                continue
            if waits.get(s, -1) < v:
                waits[s] = v
        real_waits = []
        for s, v in waits.items():
            key = (eng, s)
            if self.waited.get(key, 0) >= v:
                continue
            self.waited[key] = v
            real_waits.append((s, v))
        amt = 16 if is_dma else 1
        self.cnt[sem] += amt
        tok = Tok(sem, self.cnt[sem], eng, is_dma)
        self.q[eng].append((real_waits, fn, sem, amt))
        for b in writes:
            b.w = tok
            b.rs = []
        for b in reads:
            b.rs.append(tok)
        return tok

    def wait_all(self, eng):
        snap = dict(self.cnt)

        def fn(e):
            return None
        real_waits = []
        for s, v in snap.items():
            if s == "s_" + eng:
                continue
            key = (eng, s)
            if self.waited.get(key, 0) >= v:
                continue
            self.waited[key] = v
            real_waits.append((s, v))
        self.q[eng].append((real_waits, None, None, 0))

    def flush(self, block):
        handles = self.handles
        for eng_name in ENGS:
            ops = self.q[eng_name]
            if not ops:
                continue
            self.q[eng_name] = []
            section = getattr(block, eng_name)

            def make(ops_local):
                def body(E):
                    for waits, fn, sem, amt in ops_local:
                        for s, v in waits:
                            E.wait_ge(handles[s], v)
                        if fn is not None:
                            ins = fn(E)
                            if sem is not None:
                                ins.then_inc(handles[sem], amt)
                return body
            section(make(ops))





def emit_T(sch, t, items, f32=False):
    if f32:
        banks = (("pTf", 4), ("pb0", 4))
        ident, identB = t["ident_f"], t["ident_fB"]
    else:
        banks = (("pT16a", 8), ("pT16b", 8))
        ident, identB = t["ident_b"], t["ident_bB"]
    idx = 0
    gi = 0
    while idx < len(items):
        bname, slots = banks[gi % 2]
        bank, bankB = t[bname], t[bname + "B"]
        group = items[idx:idx + slots]
        for si, (in_ap, in_bufs, eeng, mk, erd, ewr) in enumerate(group):
            sch.emit("tensor",
                     (lambda E, in_ap=in_ap, si=si, bank=bank, ident=ident:
                      E.transpose(out=bank[:, si * 128:(si + 1) * 128],
                                  in_=in_ap, identity=ident[:])),
                     reads=list(in_bufs) + [identB], writes=[bankB])
        for si, (in_ap, in_bufs, eeng, mk, erd, ewr) in enumerate(group):
            slot_ap = bank[:, si * 128:(si + 1) * 128]
            sch.emit(eeng, mk(slot_ap), reads=list(erd) + [bankB],
                     writes=list(ewr))
        idx += slots
        gi += 1


def emit_TG(sch, t, ins_list, evacs, f32=False, gsize=None):
    if f32:
        banks = (("pTf", 4), ("pb0", 4))
        ident, identB = t["ident_f"], t["ident_fB"]
        cap = 4
    else:
        banks = (("pT16a", 8), ("pT16b", 8))
        ident, identB = t["ident_b"], t["ident_bB"]
        cap = 8
    g = gsize or cap
    assert g <= cap
    ngroups = (len(ins_list) + g - 1) // g
    assert len(evacs) == ngroups
    for gi in range(ngroups):
        bname, _ = banks[gi % 2]
        bank, bankB = t[bname], t[bname + "B"]
        group = ins_list[gi * g:(gi + 1) * g]
        for si, (in_ap, in_bufs) in enumerate(group):
            sch.emit("tensor",
                     (lambda E, in_ap=in_ap, si=si, bank=bank, ident=ident:
                      E.transpose(out=bank[:, si * 128:(si + 1) * 128],
                                  in_=in_ap, identity=ident[:])),
                     reads=list(in_bufs) + [identB], writes=[bankB])
        eeng, mk, erd, ewr = evacs[gi]
        grp_ap = bank[:, 0:len(group) * 128]
        sch.emit(eeng, mk(grp_ap), reads=list(erd) + [bankB], writes=list(ewr))


def emit_quant(sch, tiles, src_ap, out_ap, K, scr):
    nblk = K // 16
    v = "vector"
    s = "scalar"
    src3 = src_ap.rearrange("p (b k) -> p b k", k=16)
    bm, bmB = scr["bm"]
    sc, scB = scr["sc"]
    rs, rsB = scr["rs"]
    xs, xsB = scr["xs"]
    ab, abB = scr["ab"]
    i1, i1B = scr["i1"]
    i2, i2B = scr["i2"]
    m2, m2B = scr["m2"]
    m4, m4B = scr["m4"]
    sS, sB = scr["s"]
    isv, isvB = scr["isv"]
    srcB = scr["_srcB"]
    outB = scr["_outB"]
    sch.emit(v, lambda E: E.tensor_reduce(out=bm[:, 0:nblk], in_=src3, axis=AX.X,
                                          op=OP.max, apply_absolute_value=True),
             reads=[srcB], writes=[bmB])
    sch.emit(v, lambda E: E.tensor_scalar(out=sc[:, 0:nblk], in0=bm[:, 0:nblk],
                                          scalar1=1.0 / 6.0, scalar2=1e-12,
                                          op0=OP.mult, op1=OP.max),
             reads=[bmB], writes=[scB])
    sch.emit(v, lambda E: E.reciprocal(out=rs[:, 0:nblk], in_=sc[:, 0:nblk]),
             reads=[scB], writes=[rsB])
    rs3 = rs[:, 0:nblk].unsqueeze(2).broadcast_to([128, nblk, 16])
    xs3 = xs[:, 0:K].rearrange("p (b k) -> p b k", k=16)
    sch.emit(v, lambda E: E.tensor_tensor(out=xs3, in0=src3, in1=rs3, op=OP.mult),
             reads=[srcB, rsB], writes=[xsB])
    sch.emit(s, lambda E: E.activation(out=ab[:, 0:K], in_=xs[:, 0:K], func=AF.Abs),
             reads=[xsB], writes=[abB])
    sch.emit(s, lambda E: E.activation(out=i1[:, 0:K], in_=ab[:, 0:K],
                                       func=AF.Copy, scale=0.5, bias=-0.5),
             reads=[abB], writes=[i1B])
    sch.emit(s, lambda E: E.activation(out=i2[:, 0:K], in_=ab[:, 0:K],
                                       func=AF.Copy, scale=0.25, bias=-0.5),
             reads=[abB], writes=[i2B])
    sch.emit(v, lambda E: E.tensor_scalar(out=m2[:, 0:K], in0=i1[:, 0:K],
                                          scalar1=1.0, scalar2=0.5,
                                          op0=OP.min, op1=OP.mult),
             reads=[i1B], writes=[m2B])
    sch.emit(v, lambda E: E.tensor_scalar(out=m4[:, 0:K], in0=i2[:, 0:K],
                                          scalar1=1.0, scalar2=0.5,
                                          op0=OP.min, op1=OP.add),
             reads=[i2B], writes=[m4B])
    sch.emit(v, lambda E: E.tensor_tensor(out=sS[:, 0:K], in0=m2[:, 0:K],
                                          in1=m4[:, 0:K], op=OP.add),
             reads=[m2B, m4B], writes=[sB])
    sch.emit(v, lambda E: E.tensor_scalar(out=isv[:, 0:K], in0=i1[:, 0:K],
                                          scalar1=1.0, scalar2=-1.0,
                                          op0=OP.min, op1=OP.mult),
             reads=[i1B], writes=[isvB])
    sch.emit(v, lambda E: E.tensor_scalar(out=m4[:, 0:K], in0=i2[:, 0:K],
                                          scalar1=1.0, scalar2=-0.5,
                                          op0=OP.min, op1=OP.mult),
             reads=[i2B], writes=[m4B])
    sch.emit(v, lambda E: E.scalar_tensor_tensor(
        out=isv[:, 0:K], in0=isv[:, 0:K], scalar=2.0, in1=m4[:, 0:K],
        op0=OP.add, op1=OP.add),
        reads=[isvB, m4B], writes=[isvB])
    sch.emit(v, lambda E: E.tensor_tensor(out=xs[:, 0:K], in0=xs[:, 0:K],
                                          in1=isv[:, 0:K], op=OP.mult),
             reads=[xsB, isvB], writes=[xsB])
    sch.emit(s, lambda E: E.activation(out=i1[:, 0:K], in_=xs[:, 0:K],
                                       func=AF.Copy, bias=0.0),
             reads=[xsB], writes=[i1B])
    sch.emit(v, lambda E: E.tensor_tensor(out=m2[:, 0:K], in0=i1[:, 0:K],
                                          in1=sS[:, 0:K], op=OP.mult),
             reads=[i1B, sB], writes=[m2B])
    q3 = m2[:, 0:K].rearrange("p (b k) -> p b k", k=16)
    sc3 = sc[:, 0:nblk].unsqueeze(2).broadcast_to([128, nblk, 16])
    out3 = out_ap.rearrange("p (b k) -> p b k", k=16)
    sch.emit(v, lambda E: E.tensor_tensor(out=out3, in0=q3, in1=sc3, op=OP.mult),
             reads=[m2B, scB], writes=[outB])


def emit_ln_fm(sch, t, src_ap, srcB, out_ap, outB, noch, gcol=None, gcolB=None,
               bcol=None, bcolB=None):
    for o in range(noch):
        oo = o
        sch.emit("tensor", lambda E, oo=oo: E.matmul(
            out=t["pstatA"][0:1, :], lhsT=t["ones_col_f"][:, 0:1],
            rhs=src_ap[:, oo * 512:(oo + 1) * 512],
            start=(oo == 0), stop=(oo == noch - 1)),
            reads=[srcB, t["ones_col_fB"]], writes=[t["pstatAB"]])
    nd0 = float(noch * 128)
    sch.emit("vector", lambda E: E.tensor_scalar(
        out=t["st_mu"][0:1, :], in0=t["pstatA"][0:1, :], scalar1=1.0 / nd0,
        scalar2=None, op0=OP.mult), reads=[t["pstatAB"]], writes=[t["st_muB"]])
    for o in range(noch):
        oo = o
        sch.emit("scalar", lambda E, oo=oo: E.activation(
            out=t["lnsq"][:, oo * 512:(oo + 1) * 512],
            in_=src_ap[:, oo * 512:(oo + 1) * 512], func=AF.Square),
            reads=[srcB], writes=[t["lnsqB"]])
    for o in range(noch):
        oo = o
        sch.emit("tensor", lambda E, oo=oo: E.matmul(
            out=t["pstatA"][0:1, :], lhsT=t["ones_col_b"][:, 0:1],
            rhs=t["lnsq"][:, oo * 512:(oo + 1) * 512],
            start=(oo == 0), stop=(oo == noch - 1)),
            reads=[t["lnsqB"], t["ones_col_bB"]], writes=[t["pstatAB"]])
    nd = float(noch * 128)
    sch.emit("vector", lambda E: E.tensor_scalar(
        out=t["st_m2"][0:1, :], in0=t["pstatA"][0:1, :], scalar1=1.0 / nd,
        scalar2=None, op0=OP.mult), reads=[t["pstatAB"]], writes=[t["st_m2B"]])
    sch.emit("vector", lambda E: E.tensor_tensor(
        out=t["st_v"][0:1, :], in0=t["st_mu"][0:1, :], in1=t["st_mu"][0:1, :],
        op=OP.mult), reads=[t["st_muB"]], writes=[t["st_vB"]])
    sch.emit("vector", lambda E: E.tensor_tensor(
        out=t["st_v"][0:1, :], in0=t["st_m2"][0:1, :], in1=t["st_v"][0:1, :],
        op=OP.subtract), reads=[t["st_m2B"], t["st_vB"]], writes=[t["st_vB"]])
    sch.emit("scalar", lambda E: E.activation(
        out=t["st_sd"][0:1, :], in_=t["st_v"][0:1, :], func=AF.Sqrt,
        bias=t["constv"][0:1, 2:3]),
        reads=[t["st_vB"], t["constvB"]], writes=[t["st_sdB"]])
    sch.emit("vector", lambda E: E.reciprocal(
        out=t["st_r"][0:1, :], in_=t["st_sd"][0:1, :]),
        reads=[t["st_sdB"]], writes=[t["st_rB"]])
    sch.emit("tensor", lambda E: E.matmul(
        out=t["pb_mu"][:], lhsT=t["ones_row_f"][0:1, :], rhs=t["st_mu"][0:1, :],
        start=True, stop=True),
        reads=[t["st_muB"], t["ones_row_fB"]], writes=[t["pb_muB"]])
    sch.emit("tensor", lambda E: E.matmul(
        out=t["pb_r"][:], lhsT=t["ones_row_f"][0:1, :], rhs=t["st_r"][0:1, :],
        start=True, stop=True),
        reads=[t["st_rB"], t["ones_row_fB"]], writes=[t["pb_rB"]])
    for o in range(noch):
        oo = o
        sch.emit("vector", lambda E, oo=oo: E.tensor_tensor(
            out=t["sp_a"][:], in0=src_ap[:, oo * 512:(oo + 1) * 512],
            in1=t["pb_mu"][:], op=OP.subtract),
            reads=[srcB, t["pb_muB"]], writes=[t["sp_aB"]])
        if gcol is None:
            sch.emit("vector", lambda E, oo=oo: E.tensor_tensor(
                out=out_ap[:, oo * 512:(oo + 1) * 512], in0=t["sp_a"][:],
                in1=t["pb_r"][:], op=OP.mult),
                reads=[t["sp_aB"], t["pb_rB"]], writes=[outB])
        else:
            sch.emit("vector", lambda E, oo=oo: E.tensor_tensor(
                out=t["sp_a"][:], in0=t["sp_a"][:], in1=t["pb_r"][:], op=OP.mult),
                reads=[t["sp_aB"], t["pb_rB"]], writes=[t["sp_aB"]])
            sch.emit("vector", lambda E, oo=oo: E.tensor_scalar(
                out=out_ap[:, oo * 512:(oo + 1) * 512], in0=t["sp_a"][:],
                scalar1=gcol[:, oo:oo + 1], scalar2=bcol[:, oo:oo + 1],
                op0=OP.mult, op1=OP.add),
                reads=[t["sp_aB"], gcolB, bcolB], writes=[outB])



_PHASES = set(os.environ.get("MK_PHASES", "p,a,b1,b2").split(","))


def build_program():
    nc = bass.Bass("TRN2", target_bir_lowering=False)
    sch = Sched(nc)

    dram = {}

    def din(name, shape, dtype=F32):
        dram[name] = nc.dram_tensor(name, list(shape), dtype, kind="ExternalInput").ap()
        return dram[name]

    _dbg = os.environ.get("MK_DEBUG", "") == "1"

    def dscr(name, shape, dtype):
        kind = "ExternalOutput" if _dbg else "Internal"
        dram[name] = nc.dram_tensor(name, list(shape), dtype, kind=kind).ap()
        return dram[name]

    x_ap = din("x", [B_CORE, S, D])
    w_in_ap = din("W_in", [D, 2 * DI])
    b_in_ap = din("b_in_t", [128, 32])
    conv_w_ap = din("conv_w_t", [128, 64])
    conv_b_ap = din("conv_b_t", [128, 16])
    w_xp_ap = din("W_xp", [DI, 96])
    w_dt_ap = din("W_dt", [RANK, DI])
    b_dt_ap = din("b_dt_t", [128, 16])
    a_log_ap = din("A_log_re", [128, 256])
    d_skip_ap = din("D_t", [128, 16])
    w_out_ap = din("W_out", [DI, D])
    b_out_ap = din("b_out_t", [128, 8])
    ln1_g_ap = din("ln1_g_t", [128, 8])
    ln1_b_ap = din("ln1_b_t", [128, 8])
    ln2_g_ap = din("ln2_g_t", [128, 8])
    ln2_b_ap = din("ln2_b_t", [128, 8])
    w_si_ap = din("W_si", [D, D])
    b_si_ap = din("b_si_t", [128, 8])
    gates_ap = din("gates_t", [128, 24])
    alpha_ap = din("alpha_t", [128, 8])
    w_so_ap = din("W_so", [D, D])
    b_so_ap = din("b_so_t", [128, 8])
    wg_ap = din("w_gate", [FFI, D])
    wu_ap = din("w_up", [FFI, D])
    wd_ap = din("w_down", [D, FFI])
    ident_ap = din("identity", [128, 128])
    sel_ap = din("sel", [16, 2048])
    ones_ap = din("ones", [128, 2])
    ones_row_ap = din("ones_row", [1, 128])
    constv_ap = din("constv", [128, 4])

    out_ap = nc.dram_tensor("out", [B_CORE, S, D], F32, kind="ExternalOutput").ap()
    dram["out"] = out_ap

    winbf = dscr("winbf", [128, 8, 4096], BF16)
    woutbf = dscr("woutbf", [128, 16, 1024], BF16)
    wsibf = dscr("wsibf", [128, 8, 1024], BF16)
    wsobf = dscr("wsobf", [128, 8, 1024], BF16)
    wqg = dscr("wqg", [128, 8, FFI], BF16)
    wqu = dscr("wqu", [128, 8, FFI], BF16)
    wqd = dscr("wqd", [128, FF_OT, 1024], BF16)
    g_scr = dscr("g_scr", [B_CORE, 4, 128, 8, 512], BF16)
    if _dbg:
        dbg_x = dscr("dbg_x", [128, 4096], F32)
        dbg_st = dscr("dbg_st", [128, 16], F32)
        dbg_xn = dscr("dbg_xn", [128, 4096], F32)
        dbg_xq = dscr("dbg_xq", [128, 4096], BF16)
        dbg_sg = dscr("dbg_sg", [128, FF_OT * 512], BF16)
        dbg_h = dscr("dbg_h", [128, FF_OT * 512], BF16)
        dbg_hq = dscr("dbg_hq", [128, FF_OT * 512], BF16)
    uu_scr = dscr("uu_scr", [B_CORE, 8, 128, S], BF16)
    x1_scr = dscr("x1_scr", [B_CORE, 8, 128, S], BF16)

    HB = {k: Buf("hbm_" + k) for k in
          ("winbf", "woutbf", "wsibf", "wsobf", "wqg", "wqu", "wqd",
           "g_scr", "uu_scr", "x1_scr", "out", "dbg")}

    with contextlib.ExitStack() as root:
        sems = {}
        for s in SEM_NAMES:
            sems[s] = root.enter_context(nc.semaphore(s))
        sch.handles = sems
        block = root.enter_context(nc.Block())

        def sb(name, shape, dtype=F32):
            return root.enter_context(nc.sbuf_tensor(name, shape, dtype))

        t = {}

        def add(name, tens):
            t[name] = tens
            t[name + "B"] = Buf(name)
            return tens

        def addbuf(name):
            t[name + "B"] = Buf(name)

        add("ident_f", sb("ident_f", [128, 128], F32))
        add("ident_b", sb("ident_b", [128, 128], BF16))
        add("ones_col_f", sb("ones_col_f", [128, 1], F32))
        add("ones_col_b", sb("ones_col_b", [128, 1], BF16))
        add("ones_row_f", sb("ones_row_f", [1, 128], F32))
        add("constv", sb("constv_sb", [128, 4], F32))
        add("st_mu", sb("st_mu_r", [1, 512], F32))
        add("st_m2", sb("st_m2_r", [1, 512], F32))
        add("st_v", sb("st_v_r", [1, 512], F32))
        add("st_sd", sb("st_sd_r", [1, 512], F32))
        add("st_r", sb("st_r_r", [1, 512], F32))
        add("lnsq", sb("lnsq_r", [128, 4096], BF16))
        add("sp_a", sb("sp_a_r", [128, 512], F32))

        add("pb0", root.enter_context(nc.psum_tensor("pb0", [128, 512], F32)))
        add("pb1", root.enter_context(nc.psum_tensor("pb1", [128, 512], F32)))
        add("yb0", root.enter_context(nc.psum_tensor("yb0", [128, 512], F32)))
        add("yb1", root.enter_context(nc.psum_tensor("yb1", [128, 512], F32)))
        add("pT16a", root.enter_context(nc.psum_tensor("pT16a", [128, 1024], BF16)))
        add("pT16b", root.enter_context(nc.psum_tensor("pT16b", [128, 1024], BF16)))
        add("pTf", root.enter_context(nc.psum_tensor("pTf", [128, 512], F32)))
        add("pstatA", root.enter_context(nc.psum_tensor("pstatA", [1, 512], F32)))

        sch.emit("sync", lambda E: E.dma_start(out=t["ident_f"][:], in_=ident_ap),
                 writes=[t["ident_fB"]], dma_stream="s_dx")
        sch.emit("gpsimd", lambda E: E.dma_start(out=t["ident_b"][:], in_=ident_ap),
                 writes=[t["ident_bB"]], dma_stream="s_dg")
        sch.emit("sync", lambda E: E.dma_start(
            out=t["ones_col_f"][:], in_=ones_ap[:, 0:1],
            allow_slow_non_contiguous=True),
            writes=[t["ones_col_fB"]], dma_stream="s_dx")
        sch.emit("gpsimd", lambda E: E.dma_start(
            out=t["ones_col_b"][:], in_=ones_ap[:, 0:1],
            allow_slow_non_contiguous=True),
            writes=[t["ones_col_bB"]], dma_stream="s_dg2")
        sch.emit("sync", lambda E: E.dma_start(
            out=t["ones_row_f"][:], in_=ones_row_ap),
            writes=[t["ones_row_fB"]], dma_stream="s_dx")
        sch.emit("sync", lambda E: E.dma_start(
            out=t["constv"][:], in_=constv_ap),
            writes=[t["constvB"]], dma_stream="s_dx")

        with contextlib.ExitStack() as pro:
          if _PHASES & {"p", "p1", "p2", "p2a", "p2b"}:
            def psb(name, shape, dtype=F32):
                return pro.enter_context(nc.sbuf_tensor(name, shape, dtype))

            cb0 = psb("cb0", [128, 4096], BF16)
            cb1 = psb("cb1", [128, 4096], BF16)
            cB = [Buf("cb0"), Buf("cb1")]
            jobs = []
            w_in_r = w_in_ap.rearrange("(kc p) o -> kc p o", p=128)
            for kc in range(8):
                jobs.append((w_in_r[kc], winbf[:, kc, :], 4096))
            w_out_r = w_out_ap.rearrange("(kc p) o -> kc p o", p=128)
            for kc in range(16):
                jobs.append((w_out_r[kc], woutbf[:, kc, :], 1024))
            w_si_r = w_si_ap.rearrange("(kc p) o -> kc p o", p=128)
            for kc in range(8):
                jobs.append((w_si_r[kc], wsibf[:, kc, :], 1024))
            w_so_r = w_so_ap.rearrange("(kc p) o -> kc p o", p=128)
            for kc in range(8):
                jobs.append((w_so_r[kc], wsobf[:, kc, :], 1024))
            for j, (src, dst, width) in enumerate(jobs if ("p" in _PHASES or "p1" in _PHASES) else []):
                buf = [cb0, cb1][j % 2]
                bufB = cB[j % 2]
                stream = "s_dg" if j % 2 == 0 else "s_dg2"
                sch.emit("gpsimd",
                         lambda E, src=src, buf=buf, width=width: E.dma_start(
                             out=buf[:, 0:width], in_=src),
                         writes=[bufB], dma_stream=stream)
                wb = HB["winbf"] if j < 8 else (
                    HB["woutbf"] if j < 24 else (HB["wsibf"] if j < 32 else HB["wsobf"]))
                sch.emit("sync",
                         lambda E, dst=dst, buf=buf, width=width: E.dma_start(
                             out=dst, in_=buf[:, 0:width]),
                         reads=[bufB], writes=[wb], dma_stream="s_dscr")

            do_ffnq = bool(_PHASES & {"p", "p2", "p2a", "p2b"})
            qK = 1408
            scr = {}
            for nm, dt_, wdt in (("bm", F32, 176), ("sc", F32, 176), ("rs", F32, 176),
                                 ("xs", F32, 2816), ("ab", F32, 2816),
                                 ("i1", I32, 2816), ("i2", I32, 2816),
                                 ("m2", BF16, 2816), ("m4", BF16, 2816),
                                 ("s", BF16, 2816)):
                tile = psb("q_" + nm, [128, wdt], dt_)
                scr[nm] = (tile, Buf("q_" + nm))
            scr["isv"] = scr["ab"]
            wtile = psb("wtile", [128, 2816], F32)
            wtileB = Buf("wtile")
            qout = psb("qout", [128, 2816], BF16)
            qoutB = Buf("qout")
            stg = psb("stg", [128, FF_OT * 128], BF16)
            stgB = Buf("stg")

            def quant_tile(src_ap2, out_ap2, K):
                scr2 = {k: v for k, v in scr.items()}
                scr2["_srcB"] = wtileB
                scr2["_outB"] = qoutB
                nhalf = (K + qK - 1) // qK
                for hh in range(nhalf):
                    k0 = hh * qK
                    k1 = min(K, k0 + qK)
                    emit_quant(sch, t, src_ap2[:, k0:k1], out_ap2[:, k0:k1],
                               k1 - k0, scr2)

            wg_r = wg_ap.rearrange("(ot p) k -> ot p k", p=128)
            wu_r = wu_ap.rearrange("(ot p) k -> ot p k", p=128)
            wd_r = wd_ap.rearrange("(ot p) k -> ot p k", p=128)
            for wnum, (wr, dst, hbB) in enumerate(
                    (((wg_r, wqg, HB["wqg"]), (wu_r, wqu, HB["wqu"])) if do_ffnq else ())):
                for ot in range(FF_OT):
                    sch.emit("sync", lambda E, wr=wr, ot=ot: E.dma_start(
                        out=wtile[:, 0:1024], in_=wr[ot]),
                        writes=[wtileB], dma_stream="s_dw")
                    quant_tile(wtile[:, 0:1024], qout[:, 0:1024], 1024)
                    if "p2a" in _PHASES:
                        continue
                    emit_TG(sch, t,
                            [(qout[:, kb * 128:kb * 128 + 128], [qoutB])
                             for kb in range(8)],
                            [("scalar", (lambda grp: (lambda E: E.activation(
                                out=stg[:, 0:1024], in_=grp, func=AF.Copy,
                                bias=0.0))), [], [stgB])])
                    if "p2b" in _PHASES:
                        continue
                    dst_sl = dst[:, :, ot * 128:(ot + 1) * 128]
                    stg_v = stg[:, 0:8 * 128].rearrange("p (kb o) -> p kb o", o=128)
                    sch.emit("sync", lambda E, dst_sl=dst_sl, stg_v=stg_v: E.dma_start(
                        out=dst_sl, in_=stg_v),
                        reads=[stgB], writes=[hbB], dma_stream="s_dscr")
            for ot in (range(8) if do_ffnq else ()):
                sch.emit("sync", lambda E, ot=ot: E.dma_start(
                    out=wtile[:, 0:2816], in_=wd_r[ot]),
                    writes=[wtileB], dma_stream="s_dw")
                quant_tile(wtile[:, 0:2816], qout[:, 0:2816], 2816)
                if "p2a" in _PHASES:
                    continue
                emit_TG(sch, t,
                        [(qout[:, kb * 128:kb * 128 + 128], [qoutB])
                         for kb in range(FF_OT)],
                        [("scalar", (lambda grp, g0=g0, gn=gn: (lambda E: E.activation(
                            out=stg[:, g0 * 128:(g0 + gn) * 128], in_=grp,
                            func=AF.Copy, bias=0.0))), [], [stgB])
                         for g0, gn in ((0, 8), (8, 8), (16, 6))])
                if "p2b" in _PHASES:
                    continue
                dst_sl = wqd[:, :, ot * 128:(ot + 1) * 128]
                stg_v = stg[:, 0:FF_OT * 128].rearrange("p (kb o) -> p kb o", o=128)
                sch.emit("sync", lambda E, dst_sl=dst_sl, stg_v=stg_v: E.dma_start(
                    out=dst_sl, in_=stg_v),
                    reads=[stgB], writes=[HB["wqd"]], dma_stream="s_dscr")
            sch.flush(block)
        sch.fence()

        with contextlib.ExitStack() as pa:
          if "a" in _PHASES:
            def asb(name, shape, dtype=F32):
                return pa.enter_context(nc.sbuf_tensor(name, shape, dtype))

            wareA = asb("wareA", [128, 8 * FFI], BF16)
            wareAB = Buf("wareA")
            x_tm = asb("a_xtm", [128, 4096], F32)
            x_tmB = Buf("a_xtm")
            x_fmA = asb("a_xfm", [128, 4096], F32)
            x_fmAB = Buf("a_xfm")
            xn_fmA = x_tm[:]
            xn_fmAB = x_tmB
            xn_tm = asb("a_xntm", [128, 4096], F32)
            xn_tmB = Buf("a_xntm")
            xq_tm = asb("a_xqtm", [128, 1024], BF16)
            xq_tmB = Buf("a_xqtm")
            xq_fm = asb("a_xqfm", [128, 8 * 512], BF16)
            xq_fmB = Buf("a_xqfm")
            aa1 = asb("a_aa1", [128, FF_OT * 512], BF16)
            aa1B = Buf("a_aa1")
            h_fm = asb("a_hfm", [128, FF_OT * 512], BF16)
            h_fmB = Buf("a_hfm")
            h_tmh = asb("a_htmh", [128, 1408], F32)
            h_tmhB = Buf("a_htmh")
            hq_tmh = asb("a_hqtmh", [128, 1408], BF16)
            hq_tmhB = Buf("a_hqtmh")
            g_tile = xq_fm
            g_tileB = xq_fmB
            scrA = {}
            for nm, dt_, wdt in (("bm", F32, 88), ("sc", F32, 88), ("rs", F32, 88),
                                 ("xs", F32, 1408), ("ab", F32, 1408),
                                 ("i1", I32, 1408), ("i2", I32, 1408),
                                 ("m2", BF16, 1408), ("m4", BF16, 1408),
                                 ("s", BF16, 1408)):
                scrA[nm] = (asb("aq_" + nm, [128, wdt], dt_), Buf("aq_" + nm))
            scrA["isv"] = scrA["ab"]

            ltA = {
                "pstatA": t["pstatA"], "pstatAB": t["pstatAB"],
                "ones_col_f": t["ones_col_f"], "ones_col_fB": t["ones_col_fB"],
                "ones_col_b": t["ones_col_b"], "ones_col_bB": t["ones_col_bB"],
                "ones_row_f": t["ones_row_f"], "ones_row_fB": t["ones_row_fB"],
                "lnsq": t["lnsq"][:], "lnsqB": t["lnsqB"],
                "st_mu": t["st_mu"], "st_muB": t["st_muB"],
                "st_m2": t["st_m2"], "st_m2B": t["st_m2B"],
                "st_v": t["st_v"], "st_vB": t["st_vB"],
                "st_sd": t["st_sd"], "st_sdB": t["st_sdB"],
                "st_r": t["st_r"], "st_rB": t["st_rB"],
                "pb_mu": t["yb0"], "pb_muB": t["yb0B"],
                "pb_r": t["yb1"], "pb_rB": t["yb1B"],
                "sp_a": t["sp_a"], "sp_aB": t["sp_aB"],
                "constv": t["constv"], "constvB": t["constvB"],
            }
            x_r = x_ap.rearrange("b (cc j p) d -> b cc p j d", j=4, p=128)
            g_w = g_scr.rearrange("b c p o t -> b c p (o t)")

            for cc in range(NCH):
                b, ci = cc // 4, cc % 4
                sch.emit("sync", lambda E, b=b, ci=ci: E.dma_start(
                    out=x_tm[:].rearrange("p (j d) -> p j d", d=1024),
                    in_=x_r[b, ci]), writes=[x_tmB], dma_stream="s_dx")
                emit_TG(sch, t,
                        [(x_tm[:, tj * 1024 + o * 128:tj * 1024 + o * 128 + 128],
                          [x_tmB]) for o in range(8) for tj in range(4)],
                        [("scalar", (lambda grp, o=o: (lambda E: E.activation(
                            out=x_fmA[:, o * 512:o * 512 + 512], in_=grp,
                            func=AF.Copy, bias=0.0))), [], [x_fmAB])
                         for o in range(8)], f32=True)
                emit_ln_fm(sch, ltA, x_fmA[:], x_fmAB, xn_fmA, xn_fmAB, 8)
                emit_TG(sch, t,
                        [(xn_fmA[:, o * 512 + tj * 128:o * 512 + tj * 128 + 128],
                          [xn_fmAB]) for tj in range(4) for o in range(8)],
                        [("scalar", (lambda grp, tj=tj, ohalf=ohalf: (lambda E: E.activation(
                            out=xn_tm[:, tj * 1024 + ohalf * 512:tj * 1024 + ohalf * 512 + 512],
                            in_=grp, func=AF.Copy, bias=0.0))), [], [xn_tmB])
                         for tj in range(4) for ohalf in range(2)], f32=True)
                for j in range(4):
                    scr2 = {k: v for k, v in scrA.items()}
                    scr2["_srcB"] = xn_tmB
                    scr2["_outB"] = xq_tmB
                    emit_quant(sch, t, xn_tm[:, j * 1024:(j + 1) * 1024],
                               xq_tm[:, 0:1024], 1024, scr2)
                    xq_dst = xq_fm[:].rearrange(
                        "p (kb r) -> p kb r", r=512)[:, :, j * 128:j * 128 + 128]
                    emit_TG(sch, t,
                            [(xq_tm[:, kb * 128:kb * 128 + 128], [xq_tmB])
                             for kb in range(8)],
                            [("scalar", (lambda grp, xq_dst=xq_dst: (lambda E: E.activation(
                                out=xq_dst,
                                in_=grp.rearrange("p (kb r) -> p kb r", r=128),
                                func=AF.Copy, bias=0.0))), [], [xq_fmB])])
                if _dbg and cc == 0:
                    sch.emit("sync", lambda E: E.dma_start(out=dram["dbg_xn"], in_=xn_tm[:]),
                             reads=[xn_tmB], writes=[HB["dbg"]], dma_stream="s_dout")
                    sch.emit("sync", lambda E: E.dma_start(out=dram["dbg_xq"], in_=xq_fm[:]),
                             reads=[xq_fmB], writes=[HB["dbg"]], dma_stream="s_dout")
                wareA_v = wareA[:].rearrange("p (kc o) -> p kc o", o=FFI)
                sch.emit("sync", lambda E, wareA_v=wareA_v: E.dma_start(
                    out=wareA_v, in_=wqg[:, :, :]),
                    reads=[HB["wqg"]], writes=[wareAB], dma_stream="s_dw")
                for ot in range(FF_OT):
                    pb = t["pb0"] if ot % 2 == 0 else t["pb1"]
                    pbB = t["pb0B"] if ot % 2 == 0 else t["pb1B"]
                    for kc in range(8):
                        sch.emit("tensor", lambda E, ot=ot, kc=kc, pb=pb: E.matmul(
                            out=pb[:],
                            lhsT=wareA[:, kc * FFI + ot * 128:kc * FFI + ot * 128 + 128],
                            rhs=xq_fm[:, kc * 512:(kc + 1) * 512],
                            start=(kc == 0), stop=(kc == 7)),
                            reads=[wareAB, xq_fmB], writes=[pbB])
                    sch.emit("scalar", lambda E, ot=ot, pb=pb: E.activation(
                        out=aa1[:, ot * 512:(ot + 1) * 512], in_=pb[:], func=AF.Silu),
                        reads=[pbB], writes=[aa1B])
                if _dbg and cc == 0:
                    sch.emit("sync", lambda E: E.dma_start(out=dram["dbg_sg"], in_=aa1[:]),
                             reads=[aa1B], writes=[HB["dbg"]], dma_stream="s_dout")
                sch.emit("sync", lambda E, wareA_v=wareA_v: E.dma_start(
                    out=wareA_v, in_=wqu[:, :, :]),
                    reads=[HB["wqu"]], writes=[wareAB], dma_stream="s_dw")
                for ot in range(FF_OT):
                    pb = t["pb0"] if ot % 2 == 0 else t["pb1"]
                    pbB = t["pb0B"] if ot % 2 == 0 else t["pb1B"]
                    for kc in range(8):
                        sch.emit("tensor", lambda E, ot=ot, kc=kc, pb=pb: E.matmul(
                            out=pb[:],
                            lhsT=wareA[:, kc * FFI + ot * 128:kc * FFI + ot * 128 + 128],
                            rhs=xq_fm[:, kc * 512:(kc + 1) * 512],
                            start=(kc == 0), stop=(kc == 7)),
                            reads=[wareAB, xq_fmB], writes=[pbB])
                    sch.emit("vector", lambda E, ot=ot, pb=pb, pbB=pbB: E.tensor_tensor(
                        out=h_fm[:, ot * 512:(ot + 1) * 512], in0=aa1[:, ot * 512:(ot + 1) * 512],
                        in1=pb[:], op=OP.mult),
                        reads=[pbB, aa1B], writes=[h_fmB])
                if _dbg and cc == 0:
                    sch.emit("sync", lambda E: E.dma_start(out=dram["dbg_h"], in_=h_fm[:]),
                             reads=[h_fmB], writes=[HB["dbg"]], dma_stream="s_dout")
                for j in range(4):
                    for hh in range(2):
                        ots = range(11) if hh == 0 else range(11, FF_OT)
                        ots_l = list(ots)
                        n_ot = len(ots_l)
                        grps = [(0, min(8, n_ot))]
                        if n_ot > 8:
                            grps.append((8, n_ot - 8))
                        emit_TG(sch, t,
                                [(h_fm[:, ot * 512 + j * 128:ot * 512 + j * 128 + 128],
                                  [h_fmB]) for ot in ots_l],
                                [("vector", (lambda grp, g0=g0, gn=gn: (lambda E: E.tensor_copy(
                                    out=h_tmh[:, g0 * 128:(g0 + gn) * 128], in_=grp))),
                                  [], [h_tmhB]) for g0, gn in grps])
                        scr2 = {k: v for k, v in scrA.items()}
                        scr2["_srcB"] = h_tmhB
                        scr2["_outB"] = hq_tmhB
                        emit_quant(sch, t, h_tmh[:, 0:1408], hq_tmh[:, 0:1408],
                                   1408, scr2)
                        def mk_hq_evac(g0, gn, j):
                            def mk(grp):
                                dst = aa1[:].rearrange(
                                    "p (kb r) -> p kb r",
                                    r=512)[:, ots_l[g0]:ots_l[g0] + gn,
                                           j * 128:j * 128 + 128]
                                return lambda E: E.activation(
                                    out=dst,
                                    in_=grp.rearrange("p (kb r) -> p kb r", r=128),
                                    func=AF.Copy, bias=0.0)
                            return mk
                        emit_TG(sch, t,
                                [(hq_tmh[:, idx * 128:idx * 128 + 128], [hq_tmhB])
                                 for idx in range(n_ot)],
                                [("scalar", mk_hq_evac(g0, gn, j), [], [aa1B])
                                 for g0, gn in grps])
                if _dbg and cc == 0:
                    sch.emit("sync", lambda E: E.dma_start(out=dram["dbg_hq"], in_=aa1[:]),
                             reads=[aa1B], writes=[HB["dbg"]], dma_stream="s_dout")
                wareD_v = wareA[:, 0:FF_OT * 1024].rearrange("p (kc o) -> p kc o", o=1024)
                sch.emit("sync", lambda E, wareD_v=wareD_v: E.dma_start(
                    out=wareD_v, in_=wqd[:, :, :]),
                    reads=[HB["wqd"]], writes=[wareAB], dma_stream="s_dw")
                for o in range(8):
                    pb = t["pb0"] if o % 2 == 0 else t["pb1"]
                    pbB = t["pb0B"] if o % 2 == 0 else t["pb1B"]
                    for kc in range(FF_OT):
                        sch.emit("tensor", lambda E, o=o, kc=kc, pb=pb: E.matmul(
                            out=pb[:],
                            lhsT=wareA[:, kc * 1024 + o * 128:kc * 1024 + o * 128 + 128],
                            rhs=aa1[:, kc * 512:(kc + 1) * 512],
                            start=(kc == 0), stop=(kc == FF_OT - 1)),
                            reads=[wareAB, aa1B], writes=[pbB])
                    sch.emit("scalar", lambda E, o=o, pb=pb, pbB=pbB: E.activation(
                        out=g_tile[:, o * 512:(o + 1) * 512], in_=pb[:], func=AF.Silu),
                        reads=[pbB], writes=[g_tileB])
                sch.emit("sync", lambda E, b=b, ci=ci: E.dma_start(
                    out=g_w[b, ci], in_=g_tile[:]),
                    reads=[g_tileB], writes=[HB["g_scr"]], dma_stream="s_dout")
            sch.flush(block)
        sch.fence()

        with contextlib.ExitStack() as pb1:
          if "b1" in _PHASES:
            def bsb(name, shape, dtype=F32):
                return pb1.enter_context(nc.sbuf_tensor(name, shape, dtype))

            sel_sb = bsb("sel_sb", [16, 2048], BF16)
            sel_sbB = Buf("sel_sb")
            wdt_sb = bsb("wdt_sb", [RANK, DI], BF16)
            wdt_sbB = Buf("wdt_sb")
            wxp_sb = bsb("wxp_sb", [128, 16 * 96], BF16)
            wxp_sbB = Buf("wxp_sb")
            alog_sb = bsb("alog_sb", [128, 256], F32)
            alog_sbB = Buf("alog_sb")
            a_re = bsb("a_re", [128, 256], F32)
            a_reB = Buf("a_re")
            vec_cols = {}
            for nm, ap_, wdt in (("b_in", b_in_ap, 32), ("conv_w", conv_w_ap, 64),
                                 ("conv_b", conv_b_ap, 16), ("b_dt", b_dt_ap, 16),
                                 ("D_t", d_skip_ap, 16), ("b_out", b_out_ap, 8),
                                 ("ln1_g", ln1_g_ap, 8), ("ln1_b", ln1_b_ap, 8),
                                 ("ln2_g", ln2_g_ap, 8), ("ln2_b", ln2_b_ap, 8),
                                 ("b_si", b_si_ap, 8)):
                tile = bsb("c_" + nm, [128, wdt], F32)
                vec_cols[nm] = (tile, Buf("c_" + nm))
                sch.emit("sync", lambda E, tile=tile, ap_=ap_: E.dma_start(
                    out=tile[:], in_=ap_), writes=[vec_cols[nm][1]],
                    dma_stream="s_dx")
            sch.emit("gpsimd", lambda E: E.dma_start(out=sel_sb[:], in_=sel_ap),
                     writes=[sel_sbB], dma_stream="s_dg")
            sch.emit("gpsimd", lambda E: E.dma_start(out=wdt_sb[:], in_=w_dt_ap),
                     writes=[wdt_sbB], dma_stream="s_dg")
            wxp_v = wxp_sb[:].rearrange("p (kc o) -> p kc o", o=96)
            sch.emit("gpsimd", lambda E: E.dma_start(
                out=wxp_v, in_=w_xp_ap.rearrange("(kc p) o -> p kc o", p=128)),
                writes=[wxp_sbB], dma_stream="s_dg")
            sch.emit("sync", lambda E: E.dma_start(out=alog_sb[:], in_=a_log_ap),
                     writes=[alog_sbB], dma_stream="s_dx")
            sch.emit("scalar", lambda E: E.activation(out=a_re[:], in_=alog_sb[:],
                                                      func=AF.Exp),
                     reads=[alog_sbB], writes=[a_reB])
            sch.emit("vector", lambda E: E.tensor_scalar(
                out=a_re[:], in0=a_re[:], scalar1=-1.0, scalar2=None, op0=OP.mult),
                reads=[a_reB], writes=[a_reB])

            ar1 = bsb("ar1", [128, 8192], BF16)
            ar2 = bsb("ar2", [128, 8192], BF16)
            ar1B = Buf("ar1")
            ar2B = Buf("ar2")
            xn_fm = bsb("xn_fm", [128, 4096], F32)
            xn_fmB = Buf("xn_fm")
            ar4 = bsb("ar4", [128, 4096], BF16)
            ar4B = Buf("ar4")
            ar5 = bsb("ar5", [128, 16 * 515], BF16)
            ar5B = Buf("ar5")
            xc = bsb("xc", [128, 8192], BF16)
            xcB = Buf("xc")
            sz = bsb("sz", [128, 8192], BF16)
            szB = Buf("sz")
            ar8 = bsb("ar8", [128, 8192], BF16)
            ar8B = Buf("ar8")
            ar9 = bsb("ar9", [128, 8192], BF16)
            ar9B = Buf("ar9")
            ware = bsb("ware", [128, 8192], BF16)
            wareB = Buf("ware")
            uu_sb = ar4[:]
            uu_sbB = ar4B
            dA = [bsb("dA0", [128, 512], F32), bsb("dA1", [128, 512], F32)]
            dAB = [Buf("dA0"), Buf("dA1")]
            hb = [bsb("hb0", [128, 512], BF16), bsb("hb1", [128, 512], BF16)]
            hbB = [Buf("hb0"), Buf("hb1")]

            pp = [bsb("pp0", [128, 512], BF16), bsb("pp1", [128, 512], BF16)]
            ppB = [Buf("pp0"), Buf("pp1")]
            carry = bsb("carry", [128, 256], F32)
            carryB = Buf("carry")
            tails = bsb("tails", [128, 48], BF16)
            tailsB = Buf("tails")
            diag_scr = bsb("diag_scr", [128, 512], BF16)
            diag_scrB = Buf("diag_scr")
            sp_a = bsb("sp_a", [128, 512], F32)
            sp_aB = Buf("sp_a")
            sp_b = bsb("sp_b", [128, 512], F32)
            sp_bB = Buf("sp_b")
            gsc = sp_b
            gscB = sp_bB
            dtr_sb = bsb("dtr_sb", [64, 512], BF16)
            dtr_sbB = Buf("dtr_sb")
            bs_sb = bsb("bs_sb", [16, 512], BF16)
            bs_sbB = Buf("bs_sb")
            cs_sb = bsb("cs_sb", [16, 512], BF16)
            cs_sbB = Buf("cs_sb")
            lt = {
                "pstatA": t["pstatA"], "pstatAB": t["pstatAB"],
                "ones_col_f": t["ones_col_f"], "ones_col_fB": t["ones_col_fB"],
                "ones_col_b": t["ones_col_b"], "ones_col_bB": t["ones_col_bB"],
                "ones_row_f": t["ones_row_f"], "ones_row_fB": t["ones_row_fB"],
                "lnsq": t["lnsq"][:], "lnsqB": t["lnsqB"],
                "st_mu": t["st_mu"], "st_muB": t["st_muB"],
                "st_m2": t["st_m2"], "st_m2B": t["st_m2B"],
                "st_v": t["st_v"], "st_vB": t["st_vB"],
                "st_sd": t["st_sd"], "st_sdB": t["st_sdB"],
                "st_r": t["st_r"], "st_rB": t["st_rB"],
                "pb_mu": t["yb0"], "pb_muB": t["yb0B"],
                "pb_r": t["yb1"], "pb_rB": t["yb1B"],
                "sp_a": t["sp_a"], "sp_aB": t["sp_aB"],
                "constv": t["constv"], "constvB": t["constvB"],
            }
            x_r = x_ap.rearrange("b (cc j p) d -> b cc p j d", j=4, p=128)
            uu_w = uu_scr.rearrange("b i p t -> b p i t")
            x1_w = x1_scr.rearrange("b i p t -> b p i t")
            x_tm_v = ar1[:].bitcast(F32)
            x_fm_v = ar2[:].bitcast(F32)
            x1_v = ar8[:].bitcast(F32)

            for cc in range(NCH):
                b, ci = cc // 4, cc % 4
                batch_start = (ci == 0)
                sch.emit("sync", lambda E, b=b, ci=ci: E.dma_start(
                    out=x_tm_v.rearrange("p (j d) -> p j d", d=1024),
                    in_=x_r[b, ci]), writes=[ar1B], dma_stream="s_dx")
                emit_TG(sch, t,
                        [(x_tm_v[:, tj * 1024 + o * 128:tj * 1024 + o * 128 + 128],
                          [ar1B]) for o in range(8) for tj in range(4)],
                        [("scalar", (lambda grp, o=o: (lambda E: E.activation(
                            out=x_fm_v[:, o * 512:o * 512 + 512], in_=grp,
                            func=AF.Copy, bias=0.0))), [], [ar2B])
                         for o in range(8)], f32=True)
                emit_ln_fm(sch, lt, x_fm_v, ar2B, xn_fm[:], xn_fmB, 8)
                u_fm = ar4[:]
                emit_ln_fm(sch, lt, xn_fm[:], xn_fmB, u_fm, ar4B, 8,
                           gcol=vec_cols["ln1_g"][0], gcolB=vec_cols["ln1_g"][1],
                           bcol=vec_cols["ln1_b"][0], bcolB=vec_cols["ln1_b"][1])
                ware_v = ware[:].rearrange("p (kc o) -> p kc o", o=1024)
                for q in range(4):
                    sch.emit("sync", lambda E, q=q, ware_v=ware_v: E.dma_start(
                        out=ware_v, in_=winbf[:, :, q * 1024:(q + 1) * 1024]),
                        reads=[HB["winbf"]], writes=[wareB], dma_stream="s_dw")
                    for ol in range(8):
                        o = q * 8 + ol
                        pb = t["pb0"] if o % 2 == 0 else t["pb1"]
                        pbB = t["pb0B"] if o % 2 == 0 else t["pb1B"]
                        for kc in range(8):
                            sch.emit("tensor", lambda E, kc=kc, ol=ol, pb=pb: E.matmul(
                                out=pb[:],
                                lhsT=ware[:, kc * 1024 + ol * 128:kc * 1024 + ol * 128 + 128],
                                rhs=u_fm[:, kc * 512:(kc + 1) * 512],
                                start=(kc == 0), stop=(kc == 7)),
                                reads=[wareB, ar4B], writes=[pbB])
                        if o < 16:
                            sch.emit("scalar", lambda E, o=o, pb=pb, pbB=pbB: E.activation(
                                out=ar5[:, o * 515 + 3:o * 515 + 515], in_=pb[:],
                                func=AF.Identity, bias=vec_cols["b_in"][0][:, o:o + 1]),
                                reads=[pbB, vec_cols["b_in"][1]], writes=[ar5B])
                        else:
                            oz = o - 16
                            sch.emit("scalar", lambda E, o=o, oz=oz, pb=pb, pbB=pbB: E.activation(
                                out=sz[:, oz * 512:(oz + 1) * 512], in_=pb[:],
                                func=AF.Silu, bias=vec_cols["b_in"][0][:, o:o + 1]),
                                reads=[pbB, vec_cols["b_in"][1]], writes=[szB])
                if batch_start:
                    sch.emit("vector", lambda E: E.memset(tails[:], 0.0),
                             writes=[tailsB])
                for i in range(16):
                    sch.emit("vector", lambda E, i=i: E.tensor_copy(
                        out=ar5[:, i * 515:i * 515 + 3], in_=tails[:, i * 3:i * 3 + 3]),
                        reads=[tailsB], writes=[ar5B])
                for i in range(16):
                    for k in range(4):
                        sch.emit("vector", lambda E, i=i, k=k: E.tensor_scalar(
                            out=diag_scr[:, k * 128:(k + 1) * 128],
                            in0=t["ident_b"][:],
                            scalar1=vec_cols["conv_w"][0][:, i * 4 + k:i * 4 + k + 1],
                            scalar2=None, op0=OP.mult),
                            reads=[t["ident_bB"], vec_cols["conv_w"][1]],
                            writes=[diag_scrB])
                    pb = t["pb0"] if i % 2 == 0 else t["pb1"]
                    pbB = t["pb0B"] if i % 2 == 0 else t["pb1B"]
                    for k in range(4):
                        sch.emit("tensor", lambda E, i=i, k=k, pb=pb: E.matmul(
                            out=pb[:], lhsT=diag_scr[:, k * 128:(k + 1) * 128],
                            rhs=ar5[:, i * 515 + k:i * 515 + k + 512],
                            start=(k == 0), stop=(k == 3)),
                            reads=[diag_scrB, ar5B], writes=[pbB])
                    sch.emit("scalar", lambda E, i=i, pb=pb, pbB=pbB: E.activation(
                        out=xc[:, i * 512:(i + 1) * 512], in_=pb[:], func=AF.Silu,
                        bias=vec_cols["conv_b"][0][:, i:i + 1]),
                        reads=[pbB, vec_cols["conv_b"][1]], writes=[xcB])
                    sch.emit("vector", lambda E, i=i: E.tensor_copy(
                        out=tails[:, i * 3:i * 3 + 3],
                        in_=ar5[:, i * 515 + 512:i * 515 + 515]),
                        reads=[ar5B], writes=[tailsB])
                for kc in range(16):
                    sch.emit("tensor", lambda E, kc=kc: E.matmul(
                        out=t["pb0"][0:64, :],
                        lhsT=wxp_sb[:, kc * 96:kc * 96 + 64],
                        rhs=xc[:, kc * 512:(kc + 1) * 512],
                        start=(kc == 0), stop=(kc == 15)),
                        reads=[wxp_sbB, xcB], writes=[t["pb0B"]])
                for kc in range(16):
                    sch.emit("tensor", lambda E, kc=kc: E.matmul(
                        out=t["pb1"][0:16, :],
                        lhsT=wxp_sb[:, kc * 96 + 64:kc * 96 + 80],
                        rhs=xc[:, kc * 512:(kc + 1) * 512],
                        start=(kc == 0), stop=(kc == 15)),
                        reads=[wxp_sbB, xcB], writes=[t["pb1B"]])
                for kc in range(16):
                    sch.emit("tensor", lambda E, kc=kc: E.matmul(
                        out=t["pb1"][32:48, :],
                        lhsT=wxp_sb[:, kc * 96 + 80:kc * 96 + 96],
                        rhs=xc[:, kc * 512:(kc + 1) * 512],
                        start=(kc == 0), stop=(kc == 15)),
                        reads=[wxp_sbB, xcB], writes=[t["pb1B"]])
                sch.emit("scalar", lambda E: E.activation(
                    out=dtr_sb[:], in_=t["pb0"][0:64, :], func=AF.Copy, bias=0.0),
                    reads=[t["pb0B"]], writes=[dtr_sbB])
                sch.emit("scalar", lambda E: E.activation(
                    out=bs_sb[:], in_=t["pb1"][0:16, :], func=AF.Copy, bias=0.0),
                    reads=[t["pb1B"]], writes=[bs_sbB])
                sch.emit("scalar", lambda E: E.activation(
                    out=cs_sb[:], in_=t["pb1"][32:48, :], func=AF.Copy, bias=0.0),
                    reads=[t["pb1B"]], writes=[cs_sbB])
                btil = ar2[:]
                ctil = ar1[:]
                for n in range(16):
                    pb = t["pb0"] if n % 2 == 0 else t["pb1"]
                    pbB = t["pb0B"] if n % 2 == 0 else t["pb1B"]
                    sch.emit("tensor", lambda E, n=n, pb=pb: E.matmul(
                        out=pb[:], lhsT=sel_sb[:, n * 128:(n + 1) * 128],
                        rhs=bs_sb[:], start=True, stop=True),
                        reads=[sel_sbB, bs_sbB], writes=[pbB])
                    sch.emit("vector", lambda E, n=n, pb=pb, pbB=pbB: E.tensor_copy(
                        out=btil[:, n * 512:(n + 1) * 512], in_=pb[:]),
                        reads=[pbB], writes=[ar2B])
                for n in range(16):
                    pb = t["pb0"] if n % 2 == 0 else t["pb1"]
                    pbB = t["pb0B"] if n % 2 == 0 else t["pb1B"]
                    sch.emit("tensor", lambda E, n=n, pb=pb: E.matmul(
                        out=pb[:], lhsT=sel_sb[:, n * 128:(n + 1) * 128],
                        rhs=cs_sb[:], start=True, stop=True),
                        reads=[sel_sbB, cs_sbB], writes=[pbB])
                    sch.emit("vector", lambda E, n=n, pb=pb, pbB=pbB: E.tensor_copy(
                        out=ctil[:, n * 512:(n + 1) * 512], in_=pb[:]),
                        reads=[pbB], writes=[ar1B])
                dt_sb = ar8[:]
                for i in range(16):
                    pb = t["pb0"] if i % 2 == 0 else t["pb1"]
                    pbB = t["pb0B"] if i % 2 == 0 else t["pb1B"]
                    sch.emit("tensor", lambda E, i=i, pb=pb: E.matmul(
                        out=pb[:], lhsT=wdt_sb[:, i * 128:(i + 1) * 128],
                        rhs=dtr_sb[:], start=True, stop=True),
                        reads=[wdt_sbB, dtr_sbB], writes=[pbB])
                    sch.emit("scalar", lambda E, i=i, pb=pb, pbB=pbB: E.activation(
                        out=sp_a[:], in_=pb[:], func=AF.Abs,
                        bias=vec_cols["b_dt"][0][:, i:i + 1]),
                        reads=[pbB, vec_cols["b_dt"][1]], writes=[sp_aB])
                    sch.emit("scalar", lambda E: E.activation(
                        out=sp_a[:], in_=sp_a[:], func=AF.Exp, scale=-1.0),
                        reads=[sp_aB], writes=[sp_aB])
                    sch.emit("scalar", lambda E: E.activation(
                        out=sp_a[:], in_=sp_a[:], func=AF.Ln,
                        bias=t["constv"][:, 1:2]),
                        reads=[sp_aB, t["constvB"]], writes=[sp_aB])
                    sch.emit("scalar", lambda E, i=i, pb=pb, pbB=pbB: E.activation(
                        out=sp_b[:], in_=pb[:], func=AF.Relu,
                        bias=vec_cols["b_dt"][0][:, i:i + 1]),
                        reads=[pbB, vec_cols["b_dt"][1]], writes=[sp_bB])
                    sch.emit("vector", lambda E, i=i: E.tensor_tensor(
                        out=dt_sb[:, i * 512:(i + 1) * 512], in0=sp_a[:], in1=sp_b[:],
                        op=OP.add), reads=[sp_aB, sp_bB], writes=[ar8B])
                up_sb = ar9[:]
                sch.emit("vector", lambda E: E.tensor_tensor(
                    out=up_sb[:], in0=dt_sb[:], in1=xc[:], op=OP.mult),
                    reads=[ar8B, xcB], writes=[ar9B])
                yg = ar5[:]
                for i in range(16):
                    ybank = t["yb0"] if i % 2 == 0 else t["yb1"]
                    ybankB = t["yb0B"] if i % 2 == 0 else t["yb1B"]
                    for n in range(16):
                        k = i * 16 + n
                        sl = k % 2
                        sch.emit("scalar", lambda E, i=i, k=k, sl=sl: E.activation(
                            out=dA[sl][:], in_=dt_sb[:, i * 512:(i + 1) * 512],
                            func=AF.Exp, scale=a_re[:, k:k + 1]),
                            reads=[ar8B, a_reB], writes=[dAB[sl]])
                        dxb_bank = (t["pT16a"] if sl == 0 else t["pT16b"])[:].bitcast(F32)
                        dxb_bankB = t["pT16aB"] if sl == 0 else t["pT16bB"]
                        sch.emit("vector", lambda E, i=i, n=n, dxb_bank=dxb_bank:
                                 E.tensor_tensor(
                                     out=dxb_bank[:, 0:512],
                                     in0=up_sb[:, i * 512:(i + 1) * 512],
                                     in1=btil[:, n * 512:(n + 1) * 512], op=OP.mult),
                                 reads=[ar9B, ar2B], writes=[dxb_bankB])
                        if batch_start:
                            init = 0.0
                            rd = [dAB[sl]]
                        else:
                            init = carry[:, k:k + 1]
                            rd = [dAB[sl], carryB]
                        sch.emit("vector", lambda E, sl=sl, init=init,
                                 dxb_bank=dxb_bank: E.tensor_tensor_scan(
                                     out=hb[sl][:], data0=dA[sl][:],
                                     data1=dxb_bank[:, 0:512],
                                     initial=init, op0=OP.mult, op1=OP.add),
                            reads=rd + [dxb_bankB], writes=[hbB[sl]])
                        sch.emit("gpsimd", lambda E, sl=sl, n=n: E.tensor_tensor(
                            out=pp[sl][:], in0=hb[sl][:],
                            in1=ctil[:, n * 512:(n + 1) * 512], op=OP.mult),
                            reads=[hbB[sl], ar1B], writes=[ppB[sl]])
                        sch.emit("gpsimd", lambda E, sl=sl, k=k: E.tensor_copy(
                            out=carry[:, k:k + 1], in_=hb[sl][:, 511:512]),
                            reads=[hbB[sl]], writes=[carryB])
                        sch.emit("tensor", lambda E, sl=sl, n=n, ybank=ybank: E.matmul(
                            out=ybank[:], lhsT=t["ident_b"][:], rhs=pp[sl][:],
                            start=(n == 0), stop=(n == 15)),
                            reads=[ppB[sl], t["ident_bB"]], writes=[ybankB])
                    sch.emit("gpsimd", lambda E, i=i: E.tensor_scalar(
                        out=gsc[:], in0=xc[:, i * 512:(i + 1) * 512],
                        scalar1=vec_cols["D_t"][0][:, i:i + 1], scalar2=0.0,
                        op0=OP.mult, op1=OP.add),
                        reads=[xcB, vec_cols["D_t"][1]], writes=[gscB])
                    sch.emit("vector", lambda E, ybank=ybank, ybankB=ybankB: E.tensor_tensor(
                        out=gsc[:], in0=gsc[:], in1=ybank[:], op=OP.add),
                        reads=[gscB, ybankB], writes=[gscB])
                    sch.emit("vector", lambda E, i=i: E.tensor_tensor(
                        out=yg[:, i * 515 + 3:i * 515 + 515], in0=gsc[:],
                        in1=sz[:, i * 512:(i + 1) * 512], op=OP.mult),
                        reads=[gscB, szB], writes=[ar5B])
                if _dbg and cc == 1:
                    dram["dbg_carry0"] = nc.dram_tensor("dbg_carry0", [128, 256], F32,
                                                        kind="ExternalOutput").ap()
                    sch.emit("sync", lambda E: E.dma_start(
                        out=dram["dbg_carry0"], in_=carry[:]),
                        reads=[carryB], writes=[HB["dbg"]], dma_stream="s_dout")
                    for nm, tile, tb in (("dbg_xnfm", xn_fm, xn_fmB),):
                        dram[nm] = nc.dram_tensor(nm, [128, 4096], F32,
                                                  kind="ExternalOutput").ap()
                        sch.emit("sync", lambda E, nm=nm, tile=tile: E.dma_start(
                            out=dram[nm], in_=tile[:]),
                            reads=[tb], writes=[HB["dbg"]], dma_stream="s_dout")
                    dram["dbg_yg"] = nc.dram_tensor("dbg_yg", [128, 16 * 515], BF16,
                                                    kind="ExternalOutput").ap()
                    sch.emit("sync", lambda E: E.dma_start(
                        out=dram["dbg_yg"], in_=ar5[:]),
                        reads=[ar5B], writes=[HB["dbg"]], dma_stream="s_dout")
                    dram["dbg_dt"] = nc.dram_tensor("dbg_dt", [128, 8192], BF16,
                                                    kind="ExternalOutput").ap()
                    sch.emit("sync", lambda E: E.dma_start(
                        out=dram["dbg_dt"], in_=ar8[:]),
                        reads=[ar8B], writes=[HB["dbg"]], dma_stream="s_dout")
                    dram["dbg_xc"] = nc.dram_tensor("dbg_xc", [128, 8192], BF16,
                                                    kind="ExternalOutput").ap()
                    sch.emit("sync", lambda E: E.dma_start(
                        out=dram["dbg_xc"], in_=xc[:]),
                        reads=[xcB], writes=[HB["dbg"]], dma_stream="s_dout")
                wo_v = ware[:].rearrange("p (kc o) -> p kc o", o=512)
                for hh in range(2):
                    sch.emit("sync", lambda E, hh=hh, wo_v=wo_v: E.dma_start(
                        out=wo_v, in_=woutbf[:, :, hh * 512:(hh + 1) * 512]),
                        reads=[HB["woutbf"]], writes=[wareB], dma_stream="s_dw")
                    for ol in range(4):
                        o = hh * 4 + ol
                        pb = t["pb0"] if o % 2 == 0 else t["pb1"]
                        pbB = t["pb0B"] if o % 2 == 0 else t["pb1B"]
                        for kc in range(16):
                            sch.emit("tensor", lambda E, kc=kc, ol=ol, pb=pb: E.matmul(
                                out=pb[:],
                                lhsT=ware[:, kc * 512 + ol * 128:kc * 512 + ol * 128 + 128],
                                rhs=yg[:, kc * 515 + 3:kc * 515 + 515],
                                start=(kc == 0), stop=(kc == 15)),
                                reads=[wareB, ar5B], writes=[pbB])
                        sch.emit("scalar", lambda E, o=o, pb=pb, pbB=pbB: E.activation(
                            out=sp_a[:], in_=pb[:], func=AF.Identity,
                            bias=vec_cols["b_out"][0][:, o:o + 1]),
                            reads=[pbB, vec_cols["b_out"][1]], writes=[sp_aB])
                        sch.emit("vector", lambda E, o=o: E.tensor_tensor(
                            out=x1_v[:, o * 512:(o + 1) * 512], in0=sp_a[:],
                            in1=xn_fm[:, o * 512:(o + 1) * 512], op=OP.add),
                            reads=[sp_aB, xn_fmB], writes=[ar8B])
                v_sb = ar9[:, 0:4096]
                emit_ln_fm(sch, lt, x1_v, ar8B, v_sb, ar9B, 8,
                           gcol=vec_cols["ln2_g"][0], gcolB=vec_cols["ln2_g"][1],
                           bcol=vec_cols["ln2_b"][0], bcolB=vec_cols["ln2_b"][1])
                wsi_v = ware[:, 0:8 * 1024].rearrange("p (kc o) -> p kc o", o=1024)
                sch.emit("sync", lambda E, wsi_v=wsi_v: E.dma_start(
                    out=wsi_v, in_=wsibf[:, :, :]),
                    reads=[HB["wsibf"]], writes=[wareB], dma_stream="s_dw")
                for o in range(8):
                    pb = t["pb0"] if o % 2 == 0 else t["pb1"]
                    pbB = t["pb0B"] if o % 2 == 0 else t["pb1B"]
                    for kc in range(8):
                        sch.emit("tensor", lambda E, kc=kc, o=o, pb=pb: E.matmul(
                            out=pb[:],
                            lhsT=ware[:, kc * 1024 + o * 128:kc * 1024 + o * 128 + 128],
                            rhs=v_sb[:, kc * 512:(kc + 1) * 512],
                            start=(kc == 0), stop=(kc == 7)),
                            reads=[wareB, ar9B], writes=[pbB])
                    sch.emit("scalar", lambda E, o=o, pb=pb, pbB=pbB: E.activation(
                        out=uu_sb[:, o * 512:(o + 1) * 512], in_=pb[:],
                        func=AF.Identity, bias=vec_cols["b_si"][0][:, o:o + 1]),
                        reads=[pbB, vec_cols["b_si"][1]], writes=[uu_sbB])
                t0 = ci * PTOK
                sch.emit("sync", lambda E, b=b, t0=t0: E.dma_start(
                    out=uu_w[b, :, :, t0:t0 + PTOK],
                    in_=uu_sb[:].rearrange("p (i tt) -> p i tt", tt=512)),
                    reads=[uu_sbB], writes=[HB["uu_scr"]], dma_stream="s_dout")
                x1b = ar9[:, 4096:8192]
                for o in range(8):
                    sch.emit("scalar", lambda E, o=o: E.activation(
                        out=x1b[:, o * 512:(o + 1) * 512],
                        in_=x1_v[:, o * 512:(o + 1) * 512], func=AF.Copy, bias=0.0),
                        reads=[ar8B], writes=[ar9B])
                sch.emit("sync", lambda E, b=b, t0=t0: E.dma_start(
                    out=x1_w[b, :, :, t0:t0 + PTOK],
                    in_=x1b.rearrange("p (i tt) -> p i tt", tt=512)),
                    reads=[ar9B], writes=[HB["x1_scr"]], dma_stream="s_dout")
            sch.flush(block)
        sch.fence()

        with contextlib.ExitStack() as pb2:
          if "b2" in _PHASES:
            def csb(name, shape, dtype=F32):
                return pb2.enter_context(nc.sbuf_tensor(name, shape, dtype))

            ware2 = csb("ware2", [128, 8 * 1024], BF16)
            ware2B = Buf("ware2")
            uuW = csb("uuW", [128, 8 * 516], BF16)
            uuWB = Buf("uuW")
            x1ld = csb("x1ld", [128, 8 * 512], BF16)
            x1ldB = Buf("x1ld")
            gld = csb("gld", [128, 8 * 512], BF16)
            gldB = Buf("gld")
            x_tm2 = csb("x_tm2", [128, 4096], F32)
            x_tm2B = Buf("x_tm2")
            duv = csb("duv", [128, 8 * 512], BF16)
            duvB = Buf("duv")
            sgp = csb("sgp", [128, 8 * 512], BF16)
            sgpB = Buf("sgp")
            out_tm = csb("out_tm", [128, 4096], F32)
            out_tmB = Buf("out_tm")
            f1 = csb("f1", [128, 512], F32)
            f2 = csb("f2", [128, 512], F32)
            f3 = csb("f3", [128, 512], F32)
            f4 = csb("f4", [128, 512], F32)
            fk = csb("fk", [128, 512], I32)
            f1B, f2B, f3B, f4B, fkB = (Buf("f1"), Buf("f2"), Buf("f3"), Buf("f4"),
                                       Buf("fk"))
            vc2 = {}
            for nm, ap_, wdt in (("alpha", alpha_ap, 8), ("gates", gates_ap, 24),
                                 ("b_so", b_so_ap, 8)):
                tile = csb("c2_" + nm, [128, wdt], F32)
                vc2[nm] = (tile, Buf("c2_" + nm))
                sch.emit("sync", lambda E, tile=tile, ap_=ap_: E.dma_start(
                    out=tile[:], in_=ap_), writes=[vc2[nm][1]], dma_stream="s_dx")
            gp = csb("c2_gp", [128, 24], F32)
            gpB = Buf("c2_gp")
            sch.emit("vector", lambda E: E.tensor_scalar(
                out=gp[:], in0=vc2["gates"][0][:], scalar1=0.1, scalar2=None,
                op0=OP.mult), reads=[vc2["gates"][1]], writes=[gpB])
            wso_v = ware2[:].rearrange("p (kc o) -> p kc o", o=1024)
            sch.emit("sync", lambda E: E.dma_start(out=wso_v, in_=wsobf[:, :, :]),
                     reads=[HB["wsobf"]], writes=[ware2B], dma_stream="s_dw")

            x_r2 = x_ap.rearrange("b (cc j p) d -> b cc p j d", j=4, p=128)
            out_r = out_ap.rearrange("b (cc j p) d -> b cc p j d", j=4, p=128)
            uu_rd = uu_scr.rearrange("b i p t -> b p i t")
            x1_rd = x1_scr.rearrange("b i p t -> b p i t")
            g_rd = g_scr.rearrange("b c p o t -> b c p (o t)")
            uuW_v = uuW[:].rearrange("p (i w) -> p i w", w=516)

            for cc in range(NCH):
                b, ci = cc // 4, cc % 4
                t0 = ci * PTOK
                lo = max(0, t0 - 2)
                hi = min(S, t0 + PTOK + 2)
                c_off = lo - (t0 - 2)
                width = hi - lo
                sch.emit("sync", lambda E, b=b, lo=lo, width=width, c_off=c_off:
                         E.dma_start(out=uuW_v[:, :, c_off:c_off + width],
                                     in_=uu_rd[b, :, :, lo:lo + width]),
                         reads=[HB["uu_scr"]], writes=[uuWB], dma_stream="s_dscr")
                if ci == 0:
                    sch.emit("vector", lambda E: E.tensor_copy(
                        out=uuW_v[:, :, 0:2],
                        in_=uuW_v[:, :, 2:3].broadcast_to([128, 8, 2])),
                        reads=[uuWB], writes=[uuWB])
                if ci == 3:
                    sch.emit("vector", lambda E: E.tensor_copy(
                        out=uuW_v[:, :, 514:516],
                        in_=uuW_v[:, :, 513:514].broadcast_to([128, 8, 2])),
                        reads=[uuWB], writes=[uuWB])
                sch.emit("sync", lambda E, b=b, t0=t0: E.dma_start(
                    out=x1ld[:].rearrange("p (i tt) -> p i tt", tt=512),
                    in_=x1_rd[b, :, :, t0:t0 + PTOK]),
                    reads=[HB["x1_scr"]], writes=[x1ldB], dma_stream="s_dscr")
                sch.emit("sync", lambda E, b=b, ci=ci: E.dma_start(
                    out=gld[:], in_=g_rd[b, ci]),
                    reads=[HB["g_scr"]], writes=[gldB], dma_stream="s_dscr")
                sch.emit("sync", lambda E, b=b, ci=ci: E.dma_start(
                    out=x_tm2[:].rearrange("p (j d) -> p j d", d=1024),
                    in_=x_r2[b, ci]), writes=[x_tm2B], dma_stream="s_dx")
                for i in range(8):
                    base = i * 516
                    C = uuW[:, base + 2:base + 514]
                    L = uuW[:, base + 1:base + 513]
                    R = uuW[:, base + 3:base + 515]
                    LL = uuW[:, base + 0:base + 512]
                    RR = uuW[:, base + 4:base + 516]
                    sch.emit("vector", lambda E, R=R, L=L: E.tensor_tensor(
                        out=f1[:], in0=R, in1=L, op=OP.subtract),
                        reads=[uuWB], writes=[f1B])
                    sch.emit("vector", lambda E, R=R, L=L: E.tensor_tensor(
                        out=f2[:], in0=R, in1=L, op=OP.add),
                        reads=[uuWB], writes=[f2B])
                    sch.emit("vector", lambda E, C=C: E.scalar_tensor_tensor(
                        out=f3[:], in0=C, scalar=-2.0, in1=f2[:],
                        op0=OP.mult, op1=OP.add),
                        reads=[uuWB, f2B], writes=[f3B])
                    sch.emit("vector", lambda E, RR=RR, LL=LL: E.tensor_tensor(
                        out=f2[:], in0=RR, in1=LL, op=OP.subtract),
                        reads=[uuWB], writes=[f2B])
                    sch.emit("vector", lambda E: E.scalar_tensor_tensor(
                        out=f2[:], in0=f2[:], scalar=0.5, in1=f1[:],
                        op0=OP.mult, op1=OP.subtract),
                        reads=[f2B, f1B], writes=[f2B])
                    sch.emit("vector", lambda E, C=C: E.tensor_tensor(
                        out=f1[:], in0=C, in1=f1[:], op=OP.mult),
                        reads=[uuWB, f1B], writes=[f1B])
                    sch.emit("vector", lambda E: E.scalar_tensor_tensor(
                        out=f1[:], in0=f1[:], scalar=-3.0, in1=f2[:],
                        op0=OP.mult, op1=OP.subtract),
                        reads=[f1B, f2B], writes=[f1B])
                    sch.emit("vector", lambda E, C=C, i=i: E.tensor_scalar(
                        out=f4[:], in0=C, scalar1=vc2["alpha"][0][:, i:i + 1],
                        scalar2=1.0, op0=OP.mult, op1=OP.add),
                        reads=[uuWB, vc2["alpha"][1]], writes=[f4B])
                    sch.emit("vector", lambda E: E.tensor_tensor(
                        out=f3[:], in0=f4[:], in1=f3[:], op=OP.mult),
                        reads=[f4B, f3B], writes=[f3B])
                    sch.emit("vector", lambda E, C=C: E.tensor_scalar(
                        out=fk[:], in0=C, scalar1=INV_TWO_PI, scalar2=None,
                        op0=OP.mult), reads=[uuWB], writes=[fkB])
                    sch.emit("vector", lambda E, C=C: E.scalar_tensor_tensor(
                        out=f4[:], in0=fk[:], scalar=-TWO_PI, in1=C,
                        op0=OP.mult, op1=OP.add),
                        reads=[fkB, uuWB], writes=[f4B])
                    sch.emit("scalar", lambda E: E.activation(
                        out=f4[:], in_=f4[:], func=AF.Sin),
                        reads=[f4B], writes=[f4B])
                    sch.emit("vector", lambda E, i=i: E.tensor_scalar(
                        out=f2[:], in0=f4[:], scalar1=gp[:, i:i + 1], scalar2=None,
                        op0=OP.mult), reads=[f4B, gpB], writes=[f2B])
                    sch.emit("vector", lambda E, i=i: E.scalar_tensor_tensor(
                        out=f2[:], in0=f1[:], scalar=gp[:, 8 + i:8 + i + 1], in1=f2[:],
                        op0=OP.mult, op1=OP.add),
                        reads=[f1B, f2B, gpB], writes=[f2B])
                    sch.emit("vector", lambda E, i=i: E.scalar_tensor_tensor(
                        out=duv[:, i * 512:(i + 1) * 512], in0=f3[:],
                        scalar=gp[:, 16 + i:16 + i + 1], in1=f2[:],
                        op0=OP.mult, op1=OP.add),
                        reads=[f3B, f2B, gpB], writes=[duvB])
                for o in range(8):
                    pb = t["pb0"] if o % 2 == 0 else t["pb1"]
                    pbB = t["pb0B"] if o % 2 == 0 else t["pb1B"]
                    for kc in range(8):
                        sch.emit("tensor", lambda E, kc=kc, o=o, pb=pb: E.matmul(
                            out=pb[:],
                            lhsT=ware2[:, kc * 1024 + o * 128:kc * 1024 + o * 128 + 128],
                            rhs=duv[:, kc * 512:(kc + 1) * 512],
                            start=(kc == 0), stop=(kc == 7)),
                            reads=[ware2B, duvB], writes=[pbB])
                    sch.emit("scalar", lambda E, o=o, pb=pb, pbB=pbB: E.activation(
                        out=f1[:], in_=pb[:], func=AF.Identity,
                        bias=vc2["b_so"][0][:, o:o + 1]),
                        reads=[pbB, vc2["b_so"][1]], writes=[f1B])
                    sch.emit("vector", lambda E, o=o: E.tensor_tensor(
                        out=f1[:], in0=f1[:], in1=x1ld[:, o * 512:(o + 1) * 512],
                        op=OP.add), reads=[f1B, x1ldB], writes=[f1B])
                    sch.emit("vector", lambda E, o=o: E.tensor_tensor(
                        out=sgp[:, o * 512:(o + 1) * 512], in0=f1[:],
                        in1=gld[:, o * 512:(o + 1) * 512], op=OP.mult),
                        reads=[f1B, gldB], writes=[sgpB])
                emit_TG(sch, t,
                        [(sgp[:, o * 512 + tj * 128:o * 512 + tj * 128 + 128],
                          [sgpB]) for tj in range(4) for o in range(8)],
                        [("vector", (lambda grp, tj=tj: (lambda E: E.tensor_tensor(
                            out=out_tm[:, tj * 1024:tj * 1024 + 1024], in0=grp,
                            in1=x_tm2[:, tj * 1024:tj * 1024 + 1024], op=OP.add))),
                         [x_tm2B], [out_tmB]) for tj in range(4)])
                sch.emit("sync", lambda E, b=b, ci=ci: E.dma_start(
                    out=out_r[b, ci],
                    in_=out_tm[:].rearrange("p (j d) -> p j d", d=1024)),
                    reads=[out_tmB], writes=[HB["out"]], dma_stream="s_dout")
            sch.wait_all("sync")
            sch.flush(block)

    return nc



def _col(v, k):
    return np.ascontiguousarray(v.reshape(k, 128).T.astype(np.float32))


def kernel(**inp):
    x = np.asarray(inp["L0_attn_x"], dtype=np.float32)

    sel = np.zeros((16, 16, 128), dtype=np.float32)
    for n in range(16):
        sel[n, n, :] = 1.0
    sel = sel.reshape(16, 2048)
    ones = np.ones((128, 2), dtype=np.float32)
    ident = np.eye(128, dtype=np.float32)

    gates = np.asarray(inp["L0_attn_gates"], dtype=np.float32)
    alpha = np.asarray(inp["L0_attn_alpha"], dtype=np.float32)
    gates_rep = np.concatenate([
        _col(np.repeat(gates[:, j], 64), 8) for j in range(3)], axis=1)
    alpha_rep = _col(np.repeat(alpha, 64), 8)

    a_log = np.asarray(inp["L0_attn_A_log"], dtype=np.float32)
    a_log_re = np.ascontiguousarray(
        a_log.reshape(16, 128, 16).transpose(1, 0, 2).reshape(128, 256))
    conv_w = np.asarray(inp["L0_attn_conv_w"], dtype=np.float32)
    conv_w_t = np.ascontiguousarray(
        conv_w.reshape(16, 128, 4).transpose(1, 0, 2).reshape(128, 64))

    common = {
        "W_in": np.ascontiguousarray(inp["L0_attn_W_in"], dtype=np.float32),
        "b_in_t": _col(np.asarray(inp["L0_attn_b_in"], np.float32), 32),
        "conv_w_t": conv_w_t,
        "conv_b_t": _col(np.asarray(inp["L0_attn_conv_b"], np.float32), 16),
        "W_xp": np.ascontiguousarray(inp["L0_attn_W_xp"], dtype=np.float32),
        "W_dt": np.ascontiguousarray(inp["L0_attn_W_dt"], dtype=np.float32),
        "b_dt_t": _col(np.asarray(inp["L0_attn_b_dt"], np.float32), 16),
        "A_log_re": a_log_re,
        "D_t": _col(np.asarray(inp["L0_attn_D_skip"], np.float32), 16),
        "W_out": np.ascontiguousarray(inp["L0_attn_W_out"], dtype=np.float32),
        "b_out_t": _col(np.asarray(inp["L0_attn_b_out"], np.float32), 8),
        "ln1_g_t": _col(np.asarray(inp["L0_attn_ln1_g"], np.float32), 8),
        "ln1_b_t": _col(np.asarray(inp["L0_attn_ln1_b"], np.float32), 8),
        "ln2_g_t": _col(np.asarray(inp["L0_attn_ln2_g"], np.float32), 8),
        "ln2_b_t": _col(np.asarray(inp["L0_attn_ln2_b"], np.float32), 8),
        "W_si": np.ascontiguousarray(inp["L0_attn_W_si"], dtype=np.float32),
        "b_si_t": _col(np.asarray(inp["L0_attn_b_si"], np.float32), 8),
        "gates_t": gates_rep,
        "alpha_t": alpha_rep,
        "W_so": np.ascontiguousarray(inp["L0_attn_W_so"], dtype=np.float32),
        "b_so_t": _col(np.asarray(inp["L0_attn_b_so"], np.float32), 8),
        "w_gate": np.ascontiguousarray(inp["L0_ffn_w_gate"], dtype=np.float32),
        "w_up": np.ascontiguousarray(inp["L0_ffn_w_up"], dtype=np.float32),
        "w_down": np.ascontiguousarray(inp["L0_ffn_w_down"], dtype=np.float32),
        "identity": ident,
        "sel": sel,
        "ones": ones,
        "ones_row": np.ones((1, 128), dtype=np.float32),
        "constv": np.tile(np.array([[-0.5, 1.0, 1e-5, 0.0]], dtype=np.float32),
                          (128, 1)),
    }

    nc = build_program()
    in_maps = []
    for c in range(N_CORES):
        m = dict(common)
        m["x"] = np.ascontiguousarray(x[2 * c:2 * c + 2])
        in_maps.append(m)
    res = run_bass_kernel_spmd(nc, in_maps, core_ids=list(range(N_CORES)),
                               trace=(os.environ.get("MK_TRACE", "") == "1"))
    if os.environ.get("MK_TRACE", "") == "1" or os.environ.get("MK_DEBUG", "") == "1":
        global _last_results, _last_res
        _last_results = res.results
        _last_res = res
    out = np.concatenate([np.asarray(r["out"], dtype=np.float32)
                          for r in res.results], axis=0)
    return out
